# version 13
# baseline (speedup 1.0000x reference)
import contextlib

import numpy as np
import ml_dtypes

import concourse.bass as bass
import concourse.mybir as mybir
from concourse.bass import IndirectOffsetOnAxis
from concourse.bass_utils import run_bass_kernel_spmd

F32 = mybir.dt.float32
BF16 = mybir.dt.bfloat16
I32 = mybir.dt.int32
U32 = mybir.dt.uint32
U8 = mybir.dt.uint8
AF = mybir.ActivationFunctionType
ALU = mybir.AluOpType
AX = mybir.AxisListType
bf16 = ml_dtypes.bfloat16

NCORE = 8
B, S, D = 4, 4096, 2048
T = 2048
NCH = 16
SLAB = 1024
N0 = 16384
N1 = 8192
NH, NKV, DH = 32, 8, 64

GELU_A = 1.5957691216057308
GELU_B = 0.07135481283579408

ENGINES = ("sync", "tensor", "vector", "scalar", "gpsimd")


def _perm(n):
    j = np.arange(n)
    c2, r = j >> 8, j & 255
    par, p = r >> 7, r & 127
    return (256 * c2 + 2 * p + par).astype(np.int64)


PERM = _perm(D)
PERM16K = _perm(N0)
IPERM = np.argsort(PERM)


def _c(a):
    return np.ascontiguousarray(a)



def prep_host(inputs):
    h = {}
    x = np.asarray(inputs["L0_attn_hidden_states"])
    xs = []
    for r in range(NCORE):
        b, hf = r % 4, r // 4
        sl = x[b, 2048 * hf:2048 * (hf + 1), :][:, PERM]
        xs.append(_c(sl.T.reshape(NCH, 128, T)).astype(np.float32, copy=False))
    h["x"] = xs

    def pack_u8(w_packed):
        return np.asarray(w_packed).view(np.uint8)[:, ::4]

    g8 = _c(pack_u8(inputs["L0_ffn_gate_W_packed"])[PERM16K].T)
    u8_ = _c(pack_u8(inputs["L0_ffn_up_W_packed"])[PERM16K].T)
    d8 = _c(pack_u8(inputs["L0_ffn_down_W_packed"])[PERM].T)
    gsc = np.repeat(np.asarray(inputs["L0_ffn_gate_scales"])[PERM16K].T, 16, axis=0)
    usc = np.repeat(np.asarray(inputs["L0_ffn_up_scales"])[PERM16K].T, 16, axis=0)
    dsc = np.repeat(np.asarray(inputs["L0_ffn_down_scales"])[PERM].T, 16, axis=0)
    h["g8"] = [_c(g8[:, 2048 * r:2048 * (r + 1)]) for r in range(NCORE)]
    h["u8"] = [_c(u8_[:, 2048 * r:2048 * (r + 1)]) for r in range(NCORE)]
    h["d8"] = [_c(d8[1024 * r:1024 * (r + 1), :]) for r in range(NCORE)]
    h["gsc"] = [_c(gsc[:, 2048 * r:2048 * (r + 1)].astype(np.float32)) for r in range(NCORE)]
    h["usc"] = [_c(usc[:, 2048 * r:2048 * (r + 1)].astype(np.float32)) for r in range(NCORE)]
    h["dsc"] = [_c(dsc[1024 * r:1024 * (r + 1), :].astype(np.float32)) for r in range(NCORE)]

    wq = np.asarray(inputs["L0_attn_Wq"])[PERM]
    wk = np.asarray(inputs["L0_attn_Wk"])[PERM]
    wv = np.asarray(inputs["L0_attn_Wv"])[PERM]
    h["wq"] = [_c(wq[:, 256 * r:256 * (r + 1)]) for r in range(NCORE)]
    h["wkv"] = [_c(np.concatenate([wk[:, 64 * r:64 * (r + 1)],
                                   wv[:, 64 * r:64 * (r + 1)]], axis=1)) for r in range(NCORE)]
    wo_p = np.asarray(inputs["L0_attn_Wo"])[:, PERM]
    h["wo"] = [_c(wo_p[256 * r:256 * (r + 1), :]) for r in range(NCORE)]
    h["bo"] = _c(np.asarray(inputs["L0_attn_bo"])[PERM].reshape(NCH, 128).T.astype(np.float32))

    inv = 10000.0 ** (-2.0 * np.arange(32) / 64.0)
    fr = np.arange(S, dtype=np.float64)[:, None] * inv[None, :]
    cos, sin = np.cos(fr).T, np.sin(fr).T
    ctab = np.empty((64, S), np.float64)
    stab = np.empty((64, S), np.float64)
    ctab[0::2] = cos; ctab[1::2] = cos
    stab[0::2] = -sin; stab[1::2] = sin
    s8 = 1.0 / np.sqrt(8.0)
    h["rc"] = _c(np.vstack([ctab, ctab]).astype(np.float32) * np.float32(s8))
    h["rs"] = _c(np.vstack([stab, stab]).astype(np.float32) * np.float32(s8))
    slopes = 2.0 ** (-(np.arange(1, NH + 1) * 8.0 / NH))
    h["slopes"] = [
        _c(np.tile((-slopes[4 * r:4 * r + 4] / 8.0).astype(np.float32), (128, 1)))
        for r in range(NCORE)
    ]
    psw = np.zeros((128, 128), np.float32)
    for i in range(128):
        psw[i ^ 1, i] = 1.0
    h["pswapf"] = psw

    h["wg1"] = _c(np.asarray(inputs["L1_ffn_w_gate"])[PERM])
    h["wu1"] = _c(np.asarray(inputs["L1_ffn_w_up"])[PERM])
    h["wd1"] = _c(np.asarray(inputs["L1_ffn_w_down"])[:, PERM])

    h["winT"] = _c(np.asarray(inputs["L1_attn_W_in"])[:, PERM].T)
    wdtT = np.asarray(inputs["L1_attn_W_dt"])[PERM].T
    h["wdtT"] = [_c(wdtT[:, 256 * r:256 * (r + 1)]) for r in range(NCORE)]
    bdt = np.asarray(inputs["L1_attn_b_dt"])[PERM]
    h["bdt"] = [_c(bdt[256 * r:256 * (r + 1)].reshape(2, 128).T.astype(np.float32)) for r in range(NCORE)]
    al = np.asarray(inputs["L1_attn_A_log"])[PERM]
    h["alogT"] = [_c(al[256 * r:256 * (r + 1)].reshape(2, 128, 16).transpose(1, 0, 2)
                     .reshape(128, 32).astype(np.float32)) for r in range(NCORE)]
    dd = np.asarray(inputs["L1_attn_D"])[PERM]
    h["Dp"] = [_c(dd[256 * r:256 * (r + 1)].reshape(2, 128).T.astype(np.float32)) for r in range(NCORE)]

    h["ones_col"] = np.ones((128, 1), bf16)
    h["ones_row"] = np.ones((1, 128), bf16)
    h["ident"] = np.eye(128, dtype=np.float32).astype(bf16)
    cst = np.zeros((128, 8), np.float32)
    cst[:, 0] = 1e-5
    cst[:, 1] = -4.0
    cst[:, 2] = -12.0
    h["consts"] = cst
    h["idxbase"] = [_c((256 * r + np.arange(128))[:, None].astype(np.int32)) for r in range(NCORE)]
    oh = np.zeros((32, 32, 128), np.float32)
    for j in range(32):
        oh[j, j, :] = 1.0
    h["onehot32"] = _c(oh.reshape(32, 32 * 128).astype(bf16))
    return h


def unprep_host(stream_shards):
    out = np.empty((B, S, D), np.float32)
    for r in range(NCORE):
        b, hf = r % 4, r // 4
        sl = np.asarray(stream_shards[r]).reshape(D, T).T
        out[b, 2048 * hf:2048 * (hf + 1), :] = sl[:, IPERM]
    return out



class Sem:
    __slots__ = ("h", "n", "base")

    def __init__(self, h, base=0):
        self.h = h
        self.n = base
        self.base = base

    def inc(self, instr, k=1):
        instr.then_inc(self.h, k)
        self.n += k
        return instr

    def dinc(self, instr):
        return self.inc(instr, 16)

    def wait(self, eng, v):
        eng.wait_ge(self.h, self.base + v)

    def wait_abs(self, eng):
        eng.wait_ge(self.h, self.n)


class Ctx:
    def __init__(self, nc, stack):
        self.nc = nc
        self.stack = stack
        self.block = stack.enter_context(nc.Block())
        self._names = {}
        self.s_bar = self.sem("bar")
        self.scr = self.sb("barscr", [1, 16], F32)

    def name(self, base):
        k = self._names.get(base, 0)
        self._names[base] = k + 1
        return f"{base}_{k}"

    def sem(self, base):
        pool = getattr(self, "_sempool", None)
        if pool is None:
            pool = self._sempool = {}
        prev = pool.get(base)
        if prev is None:
            s = Sem(self.stack.enter_context(self.nc.semaphore(self.name(base))))
        else:
            s = Sem(prev.h, base=prev.n)
        pool[base] = s
        return s

    def gsem(self, base):
        return Sem(self.stack.enter_context(self.nc.semaphore(self.name(base))))

    def sb(self, base, shape, dtype, stack=None):
        st = stack or getattr(self, "default_stack", None) or self.stack
        return st.enter_context(self.nc.sbuf_tensor(self.name(base), shape, dtype))

    def ps(self, base, shape, dtype=F32, stack=None):
        st = stack or getattr(self, "default_stack", None) or self.stack
        return st.enter_context(self.nc.psum_tensor(self.name(base), shape, dtype))

    def barrier(self):
        sb = self.s_bar
        target = sb.n + 5
        block = self.block
        scr = self.scr

        @block.sync
        def _(e):
            sb.inc(e.nop(), 1)
            e.wait_ge(sb.h, target)

        @block.tensor
        def _(e):
            sb.inc(e.nop(), 1)
            e.wait_ge(sb.h, target)

        @block.vector
        def _(e):
            sb.inc(e.memset(scr[:, 0:1], 0.0), 1)
            e.wait_ge(sb.h, target)

        @block.scalar
        def _(e):
            sb.inc(e.activation(scr[:, 1:2], scr[:, 1:2], AF.Identity), 1)
            e.wait_ge(sb.h, target)

        @block.gpsimd
        def _(e):
            sb.inc(e.memset(scr[:, 2:3], 0.0), 1)
            e.wait_ge(sb.h, target)

        assert sb.n == target



def emit_stream_copy(ctx, xs_in, xs_out):
    block = ctx.block
    with contextlib.ExitStack() as lst:
        bt = [ctx.sb("cp_b", [128, T], F32, lst) for _ in range(2)]
        s_l = ctx.sem("cpl")
        s_s = ctx.sem("cps")

        @block.sync
        def _(sync):
            for c in range(NCH):
                if c >= 2:
                    s_s.wait(sync, 16 * (c - 1))
                s_l.dinc(sync.dma_start(out=bt[c % 2][:, :], in_=xs_in[c, :, :]))

        @block.gpsimd
        def _(g):
            for c in range(NCH):
                s_l.wait(g, 16 * (c + 1))
                s_s.dinc(g.dma_start(out=xs_out[c, :, :], in_=bt[c % 2][:, :]))
            s_s.wait(g, 16 * NCH)
    ctx.barrier()



def emit_dequant(ctx, io):
    nc, block = ctx.nc, ctx.block
    with contextlib.ExitStack() as lst:
        vt = [ctx.sb("dq_v", [128, 2048], U8, lst) for _ in range(2)]
        sct = [ctx.sb("dq_sc", [128, 2048], F32, lst) for _ in range(2)]
        v32 = ctx.sb("dq_v32", [128, 2048], I32, lst)
        n32 = ctx.sb("dq_n32", [128, 2048], I32, lst)
        cf = ctx.sb("dq_cf", [128, 2048], F32, lst)
        bt_ = ctx.sb("dq_b", [128, 2048], F32, lst)
        ut = [ctx.sb("dq_u", [128, 2048], F32, lst) for _ in range(2)]
        zt = [ctx.sb("dq_z", [128, 2048], F32, lst) for _ in range(2)]
        r4t = [ctx.sb("dq_r4", [128, 2048], F32, lst) for _ in range(2)]
        r6t = [ctx.sb("dq_r6", [128, 2048], F32, lst) for _ in range(2)]
        tq = ctx.sb("dq_t", [128, 2048], F32, lst)
        mt = [ctx.sb("dq_m", [128, 2048], F32, lst) for _ in range(2)]
        wt = [ctx.sb("dq_w", [128, 2048], BF16, lst) for _ in range(2)]

        s_l = ctx.sem("dql")
        s_v = ctx.sem("dqv")
        s_s = ctx.sem("dqs")
        s_p = ctx.sem("dqp")
        s_w = ctx.sem("dqw")
        s_c = ctx.sem("dqc")
        s_ag = [ctx.gsem("dqag") for _ in range(3)]

        tiles = [(m, c2) for m in range(3) for c2 in range(8)]

        @block.sync
        def _(sync):
            for idx, (m, c2) in enumerate(tiles):
                if idx >= 2:
                    s_w.wait(sync, 2 * (idx - 1))
                src = (io["g8"], io["u8"], io["d8"])[m]
                ssc = (io["gsc"], io["usc"], io["dsc"])[m]
                s_l.dinc(sync.dma_start(out=vt[idx % 2][:, :], in_=src[128 * c2:128 * (c2 + 1), :]))
                s_l.dinc(sync.dma_start(out=sct[idx % 2][:, :], in_=ssc[128 * c2:128 * (c2 + 1), :]))

        def cum_contrib(idx):
            g_ = 4 * min(idx, 8) + 4 * min(max(idx - 8, 0), 8) + 2 * max(idx - 16, 0)
            return 16 * g_

        @block.vector
        def _(v):
            for idx, (m, c2) in enumerate(tiles):
                par = idx % 2
                s_l.wait(v, 32 * (idx + 1))
                if idx >= 1:
                    s_c.wait(v, cum_contrib(idx))
                v.tensor_copy(v32[:, :], vt[par][:, :])
                for nib in range(2):
                    j = 2 * idx + nib
                    if nib == 0:
                        v.tensor_scalar(n32[:, :], v32[:, :], 15, None, ALU.bitwise_and)
                    else:
                        v.tensor_scalar(n32[:, :], v32[:, :], 4, None, ALU.logical_shift_right)
                    v.tensor_copy(cf[:, :], n32[:, :])
                    v.tensor_scalar(bt_[:, :], cf[:, :], 8.0, None, ALU.is_ge)
                    v.scalar_tensor_tensor(ut[nib][:, :], bt_[:, :], -8.0, cf[:, :], ALU.mult, ALU.add)
                    s_v.inc(v.tensor_scalar(zt[nib][:, :], bt_[:, :], -1.0, 0.5, ALU.mult, ALU.add), 1)
                    s_p.wait(v, j + 1)
                    s_w.inc(v.tensor_tensor(wt[nib][:, :], mt[nib][:, :], sct[par][:, :], ALU.mult), 1)

        @block.scalar
        def _(sc):
            for idx in range(len(tiles)):
                for nib in range(2):
                    j = 2 * idx + nib
                    s_v.wait(sc, j + 1)
                    if j >= 2:
                        s_p.wait(sc, j - 1)
                    sc.activation(r4t[nib][:, :], ut[nib][:, :], AF.Relu, bias=io["consts_t"][:, 1:2])
                    s_s.inc(sc.activation(r6t[nib][:, :], ut[nib][:, :], AF.Relu,
                                          bias=io["consts_t"][:, 2:3], scale=2.0), 1)

        @block.gpsimd
        def _(g):
            for idx, (m, c2) in enumerate(tiles):
                for nib in range(2):
                    j = 2 * idx + nib
                    s_s.wait(g, j + 1)
                    g.tensor_tensor(tq[:, :], ut[nib][:, :], r4t[nib][:, :], ALU.add)
                    g.tensor_tensor(tq[:, :], tq[:, :], r6t[nib][:, :], ALU.add)
                    s_p.inc(g.tensor_tensor(mt[nib][:, :], tq[:, :], zt[nib][:, :], ALU.mult), 1)
                s_w.wait(g, 2 * (idx + 1))
                for nib in range(2):
                    kc = 2 * c2 + nib
                    if m < 2:
                        ct = io["ctg"] if m == 0 else io["ctu"]
                        for sl_ in range(2):
                            s_c.dinc(g.dma_start(
                                out=ct[sl_, :, kc, :],
                                in_=wt[nib][:, 1024 * sl_:1024 * (sl_ + 1)]))
                    else:
                        sl_, ic = kc // 8, kc % 8
                        s_c.dinc(g.dma_start(out=io["ctd"][sl_, :, ic, :], in_=wt[nib][:, :]))

            s_c.wait_abs(g)
            pairs = [(io["ctg"], io["cg"]), (io["ctu"], io["cu"]), (io["ctd"], io["cd"])]
            for i, (src, dst) in enumerate(pairs):
                s_ag[i].inc(g.collective_compute(
                    "AllGather", ALU.bypass,
                    replica_groups=[list(range(NCORE))],
                    ins=[src.ap().opt()], outs=[dst.ap().opt()]), 1)
            for i in range(3):
                s_ag[i].wait(g, 1)
    ctx.barrier()



def emit_ln(ctx, io, xs_in, xn, xoff, t0, tn):
    nc, block = ctx.nc, ctx.block
    ntt = tn // 512
    with contextlib.ExitStack() as lst:
        xf = ctx.sb("ln_xf", [128, NCH, tn], F32, lst)
        xb = [ctx.sb("ln_xb", [128, tn], BF16, lst) for _ in range(2)]
        xq = [ctx.sb("ln_xq", [128, tn], BF16, lst) for _ in range(2)]
        rows = ctx.sb("ln_rows", [1, 6 * tn], F32, lst)
        m16 = ctx.sb("ln_m16", [1, tn], BF16, lst)
        r16 = ctx.sb("ln_r16", [1, tn], BF16, lst)
        mb = ctx.sb("ln_mb", [128, tn], F32, lst)
        rb = ctx.sb("ln_rb", [128, tn], F32, lst)
        tmpf = ctx.sb("ln_tmp", [128, tn], F32, lst)
        pstat = [ctx.ps("ln_ps", [1, 512], F32, lst) for _ in range(2 * ntt)]
        pbc = [ctx.ps("ln_pbc", [128, 512], F32, lst) for _ in range(2)]

        s_x = ctx.sem("lnx")
        s_cast = ctx.sem("lncast")
        s_mm = ctx.sem("lnmm")
        s_row = ctx.sem("lnrow")
        s_act = ctx.sem("lnact")
        s_bcm = ctx.sem("lnbcm")
        s_bce = ctx.sem("lnbce")
        s_xn = ctx.sem("lnxn")

        SUM = [pstat[tt] for tt in range(ntt)]
        SQ = [pstat[ntt + tt] for tt in range(ntt)]
        MEAN = rows[:, 2 * tn:3 * tn]
        VAR = rows[:, 3 * tn:4 * tn]
        TMP = rows[:, 4 * tn:5 * tn]
        RSTD = rows[:, 5 * tn:6 * tn]

        @block.sync
        def _(sync):
            for c in range(NCH):
                s_x.dinc(sync.dma_start(out=xf[:, c, :], in_=xs_in[c, :, t0:t0 + tn]))

        @block.scalar
        def _(sc):
            for c in range(NCH):
                s_x.wait(sc, 16 * (c + 1))
                if c >= 2:
                    s_mm.wait(sc, c - 1)
                sc.activation(xb[c % 2][:, :], xf[:, c, :], AF.Identity)
                s_cast.inc(sc.activation(xq[c % 2][:, :], xf[:, c, :], AF.Square), 1)

        @block.tensor
        def _(te):
            for c in range(NCH):
                s_cast.wait(te, c + 1)
                last = None
                for tt in range(ntt):
                    te.matmul(SUM[tt][:, :], io["ones_col_t"][:, :],
                              xb[c % 2][:, 512 * tt:512 * (tt + 1)],
                              start=(c == 0), stop=(c == NCH - 1))
                    last = te.matmul(SQ[tt][:, :], io["ones_col_t"][:, :],
                                     xq[c % 2][:, 512 * tt:512 * (tt + 1)],
                                     start=(c == 0), stop=(c == NCH - 1))
                s_mm.inc(last, 1)

        @block.vector
        def _(v):
            s_mm.wait(v, NCH)
            for tt in range(ntt):
                sl_ = slice(512 * tt, 512 * (tt + 1))
                v.tensor_scalar(MEAN[:, sl_], SUM[tt][:, :], 1.0 / D, None, ALU.mult)
                v.tensor_scalar(TMP[:, sl_], SQ[tt][:, :], 1.0 / D, None, ALU.mult)
            v.tensor_tensor(VAR[:, :], MEAN[:, :], MEAN[:, :], ALU.mult)
            s_row.inc(v.tensor_tensor(VAR[:, :], TMP[:, :], VAR[:, :], ALU.subtract), 1)

        @block.scalar
        def _(sc):
            s_row.wait(sc, 1)
            sc.activation(TMP[:, :], VAR[:, :], AF.Ln, bias=io["consts_t"][0:1, 0:1])
            sc.activation(RSTD[:, :], TMP[:, :], AF.Exp, scale=-0.5)
            sc.activation(m16[:, :], MEAN[:, :], AF.Identity)
            s_act.inc(sc.activation(r16[:, :], RSTD[:, :], AF.Identity), 1)

        @block.tensor
        def _(te):
            s_act.wait(te, 1)
            jobs = []
            for tt in range(ntt):
                jobs.append((m16[:, 512 * tt:512 * (tt + 1)], tt))
            for tt in range(ntt):
                jobs.append((r16[:, 512 * tt:512 * (tt + 1)], ntt + tt))
            for j, (src, _) in enumerate(jobs):
                if j >= 2:
                    s_bce.wait(te, j - 1)
                s_bcm.inc(te.matmul(pbc[j % 2][:, :], io["ones_row_t"][:, :], src,
                                    start=True, stop=True), 1)

        @block.scalar
        def _(sc):
            for j in range(2 * ntt):
                s_bcm.wait(sc, j + 1)
                dst = mb if j < ntt else rb
                tt = j % ntt
                s_bce.inc(sc.activation(dst[:, 512 * tt:512 * (tt + 1)],
                                        pbc[j % 2][:, :], AF.Identity), 1)

        @block.vector
        def _(v):
            s_bce.wait(v, 2 * ntt)
            for c in range(NCH):
                v.tensor_tensor(tmpf[:, :], xf[:, c, :], mb[:, :], ALU.subtract)
                s_xn.inc(v.tensor_tensor(xn[:, c, xoff:xoff + tn], tmpf[:, :], rb[:, :],
                                         ALU.mult), 1)
    ctx.barrier()



def emit_ffn0_slab(ctx, io, xn, xs_out, t0):
    nc, block = ctx.nc, ctx.block
    NSUP = 16
    with contextlib.ExitStack() as lst:
        WG = ctx.sb("f0_wg", [128, NCH, 1024], BF16, lst)
        WU = ctx.sb("f0_wu", [128, NCH, 1024], BF16, lst)
        WD = ctx.sb("f0_wd", [128, 8, 2048], BF16, lst)
        INT = [ctx.sb("f0_int", [128, 8, 1024], BF16, lst) for _ in range(2)]
        gof = [ctx.sb("f0_gof", [128, 512], F32, lst) for _ in range(2)]
        us = [ctx.sb("f0_us", [128, 512], F32, lst) for _ in range(2)]
        sqt = [ctx.sb("f0_sq", [128, 512], F32, lst) for _ in range(2)]
        at = [ctx.sb("f0_at", [128, 512], F32, lst) for _ in range(2)]
        y2t = [ctx.sb("f0_y2", [128, 512], F32, lst) for _ in range(2)]
        sgt = [ctx.sb("f0_sg", [128, 512], F32, lst) for _ in range(2)]
        gut = [ctx.sb("f0_gu", [128, 512], F32, lst) for _ in range(2)]
        stg = [ctx.sb("f0_stg", [128, 512], F32, lst) for _ in range(4)]
        psG = [ctx.ps("f0_pg", [128, 512], F32, lst) for _ in range(2)]
        psU = [ctx.ps("f0_pu", [128, 512], F32, lst) for _ in range(2)]
        psD = [ctx.ps("f0_pd", [128, 512], F32, lst) for _ in range(2)]

        s_w = ctx.sem("f0w")
        s_phA = ctx.sem("f0A")
        s_phB = ctx.sem("f0B")
        s_ev = ctx.sem("f0ev")
        s_sq = ctx.sem("f0sq")
        s_y2 = ctx.sem("f0y2")
        s_sg = ctx.sem("f0sg")
        s_int = ctx.sem("f0int")
        s_fc = ctx.sem("f0fc")
        s_fd = ctx.sem("f0fd")

        TTn = 2

        @block.sync
        def _(sync):
            for sp in range(NSUP):
                if sp >= 1:
                    s_phA.wait(sync, 16 * sp)
                s_w.dinc(sync.dma_start(out=WG[:, :, :], in_=io["cg"][sp, :, :, :]))
                s_w.dinc(sync.dma_start(out=WU[:, :, :], in_=io["cu"][sp, :, :, :]))
                if sp >= 1:
                    s_phB.wait(sync, 32 * sp)
                s_w.dinc(sync.dma_start(out=WD[:, :, :], in_=io["cd"][sp, :, :, :]))

        @block.tensor
        def _(te):
            for sp in range(NSUP):
                s_w.wait(te, 48 * sp + 32)
                for idx in range(16):
                    ncc, tt = idx // TTn, idx % TTn
                    par = idx % 2
                    gidx = 16 * sp + idx
                    if gidx >= 2:
                        s_ev.wait(te, gidx - 1)
                    for kc in range(NCH):
                        te.matmul(psG[par][:, :], WG[:, kc, 128 * ncc:128 * (ncc + 1)],
                                  xn[:, kc, 512 * tt:512 * (tt + 1)],
                                  start=(kc == 0), stop=(kc == NCH - 1))
                    last = None
                    for kc in range(NCH):
                        last = te.matmul(psU[par][:, :], WU[:, kc, 128 * ncc:128 * (ncc + 1)],
                                         xn[:, kc, 512 * tt:512 * (tt + 1)],
                                         start=(kc == 0), stop=(kc == NCH - 1))
                    s_phA.inc(last, 1)
                s_w.wait(te, 48 * (sp + 1))
                s_int.wait(te, 16 * (sp + 1))
                for idx in range(32):
                    hc, tt = idx // TTn, idx % TTn
                    par = idx % 2
                    gidx = 32 * sp + idx
                    if gidx >= 2:
                        s_fc.wait(te, gidx - 1)
                    last = None
                    for ic in range(8):
                        last = te.matmul(psD[par][:, :], WD[:, ic, 128 * hc:128 * (hc + 1)],
                                         INT[sp % 2][:, ic, 512 * tt:512 * (tt + 1)],
                                         start=(ic == 0), stop=(ic == 7))
                    s_phB.inc(last, 1)

        @block.scalar
        def _(sc):
            for sp in range(NSUP):
                for idx in range(16):
                    par = idx % 2
                    gidx = 16 * sp + idx
                    s_phA.wait(sc, gidx + 1)
                    if gidx >= 2:
                        s_int.wait(sc, gidx - 1)
                    sc.activation(gof[par][:, :], psG[par][:, :], AF.Identity)
                    s_ev.inc(sc.activation(us[par][:, :], psU[par][:, :], AF.Identity), 1)
                    s_sq.inc(sc.activation(sqt[par][:, :], gof[par][:, :], AF.Square), 1)
                    s_y2.wait(sc, gidx + 1)
                    s_sg.inc(sc.activation(sgt[par][:, :], y2t[par][:, :], AF.Sigmoid), 1)
                for idx in range(32):
                    par = idx % 2
                    gidx = 32 * sp + idx
                    s_phB.wait(sc, gidx + 1)
                    if gidx >= 4:
                        s_fd.wait(sc, 16 * (gidx - 3))
                    s_fc.inc(sc.activation(stg[gidx % 4][:, :], psD[par][:, :],
                                           AF.Identity, scale=0.5), 1)

        @block.vector
        def _(v):
            for sp in range(NSUP):
                for idx in range(16):
                    par = idx % 2
                    gidx = 16 * sp + idx
                    s_sq.wait(v, gidx + 1)
                    v.tensor_scalar(at[par][:, :], sqt[par][:, :], GELU_B, GELU_A,
                                    ALU.mult, ALU.add)
                    s_y2.inc(v.tensor_tensor(y2t[par][:, :], at[par][:, :],
                                             gof[par][:, :], ALU.mult), 1)

        @block.gpsimd
        def _(g):
            for sp in range(NSUP):
                if sp >= 2:
                    s_phB.wait(g, 32 * (sp - 1))
                for idx in range(16):
                    ncc, tt = idx // TTn, idx % TTn
                    par = idx % 2
                    gidx = 16 * sp + idx
                    s_sg.wait(g, gidx + 1)
                    g.tensor_tensor(gut[par][:, :], gof[par][:, :], us[par][:, :], ALU.mult)
                    s_int.inc(g.tensor_tensor(INT[sp % 2][:, ncc, 512 * tt:512 * (tt + 1)],
                                              gut[par][:, :], sgt[par][:, :], ALU.mult), 1)
                for idx in range(32):
                    hc, tt = idx // TTn, idx % TTn
                    gidx = 32 * sp + idx
                    s_fc.wait(g, gidx + 1)
                    s_fd.dinc(g.dma_start(
                        out=xs_out[hc, :, t0 + 512 * tt:t0 + 512 * (tt + 1)],
                        in_=stg[gidx % 4][:, :], accum_op=ALU.add))
            s_fd.wait_abs(g)
    ctx.barrier()



def emit_ffn1_slab(ctx, io, xn, xs_out, t0, parts="ATB"):
    nc, block = ctx.nc, ctx.block
    NSUP = 8
    with contextlib.ExitStack() as lst:
        WG = ctx.sb("f1_wg", [128, NCH, 1024], BF16, lst)
        WU = ctx.sb("f1_wu", [128, NCH, 1024], BF16, lst)
        WD = ctx.sb("f1_wd", [128, 8, 2048], BF16, lst)
        HS = ctx.sb("f1_hs", [128, 8, 1024], BF16, lst)
        IT = ctx.sb("f1_it", [128, 8, SLAB], BF16, lst)
        sgt = [ctx.sb("f1_sg", [128, 512], F32, lst) for _ in range(2)]
        uft = [ctx.sb("f1_uf", [128, 512], F32, lst) for _ in range(2)]
        ht = [ctx.sb("f1_h", [128, 512], BF16, lst) for _ in range(2)]
        abt = [ctx.sb("f1_ab", [128, 512], BF16, lst) for _ in range(2)]
        mm1 = [ctx.sb("f1_m1", [128, 128], BF16, lst) for _ in range(2)]
        mm2 = [ctx.sb("f1_m2", [128, 128], BF16, lst) for _ in range(2)]
        t2t = [ctx.sb("f1_t2", [128, 128], BF16, lst) for _ in range(2)]
        get_ = [ctx.sb("f1_ge", [128, 512], BF16, lst) for _ in range(2)]
        stg = [ctx.sb("f1_stg", [128, 512], F32, lst) for _ in range(4)]
        psG = [ctx.ps("f1_pg", [128, 512], F32, lst) for _ in range(2)]
        psU = [ctx.ps("f1_pu", [128, 512], F32, lst) for _ in range(2)]
        psD = [ctx.ps("f1_pd", [128, 512], F32, lst) for _ in range(2)]
        psT = [ctx.ps("f1_pt", [128, 128], BF16, lst) for _ in range(2)]

        s_w = ctx.sem("f1w")
        s_phA = ctx.sem("f1A")
        s_phB = ctx.sem("f1B")
        s_ev = ctx.sem("f1ev")
        s_h = ctx.sem("f1h")
        s_tr = ctx.sem("f1tr")
        s_te = ctx.sem("f1te")
        s_fc = ctx.sem("f1fc")
        s_fd = ctx.sem("f1fd")

        @block.gpsimd
        def _(g):
            for sp in range(NSUP + 1):
                if sp < NSUP and ("A" in parts or "W" in parts):
                    if sp >= 1 and "A" in parts:
                        s_phA.wait(g, 16 * sp)
                    for kc in range(NCH):
                        s_w.dinc(g.dma_start(out=WG[:, kc, :],
                                             in_=io["wg1"][128 * kc:128 * (kc + 1),
                                                           1024 * sp:1024 * (sp + 1)]))
                    for kc in range(NCH):
                        s_w.dinc(g.dma_start(out=WU[:, kc, :],
                                             in_=io["wu1"][128 * kc:128 * (kc + 1),
                                                           1024 * sp:1024 * (sp + 1)]))
                if sp >= 1 and "B" in parts:
                    spf = sp - 1
                    for idx in range(32):
                        hc, tt2 = idx // 2, idx % 2
                        gidx = 32 * spf + idx
                        s_fc.wait(g, gidx + 1)
                        s_fd.dinc(g.dma_start(
                            out=xs_out[hc, :, t0 + 512 * tt2:t0 + 512 * (tt2 + 1)],
                            in_=stg[gidx % 4][:, :], accum_op=ALU.add))
                if sp < NSUP and "B" in parts:
                    if sp >= 1:
                        s_phB.wait(g, 32 * sp)
                    for ic in range(8):
                        s_w.dinc(g.dma_start(
                            out=WD[:, ic, :],
                            in_=io["wd1"][1024 * sp + 128 * ic:1024 * sp + 128 * (ic + 1), :]))
            s_fd.wait_abs(g)

        @block.tensor
        def _(te):
            for sp in range(NSUP):
                if "A" not in parts:
                    continue
                s_w.wait(te, 16 * ((40 if "B" in parts else 32) * sp + 32))
                for idx in range(16):
                    tc, it = idx // 2, idx % 2
                    par = idx % 2
                    gidx = 16 * sp + idx
                    if gidx >= 2:
                        s_ev.wait(te, gidx - 1)
                    for kc in range(NCH):
                        te.matmul(psG[par][:, :], xn[:, kc, 128 * tc:128 * (tc + 1)],
                                  WG[:, kc, 512 * it:512 * (it + 1)],
                                  start=(kc == 0), stop=(kc == NCH - 1))
                    last = None
                    for kc in range(NCH):
                        last = te.matmul(psU[par][:, :], xn[:, kc, 128 * tc:128 * (tc + 1)],
                                         WU[:, kc, 512 * it:512 * (it + 1)],
                                         start=(kc == 0), stop=(kc == NCH - 1))
                    s_phA.inc(last, 1)
                if "T" not in parts:
                    continue
                s_h.wait(te, 16 * (sp + 1))
                for j in range(64):
                    tc, ic = j // 8, j % 8
                    gj = 64 * sp + j
                    if gj >= 2:
                        s_te.wait(te, gj - 1)
                    s_tr.inc(te.transpose(psT[j % 2][:, :], HS[:, tc, 128 * ic:128 * (ic + 1)],
                                          io["ident_t"][:, :]), 1)
                if "B" not in parts:
                    continue
                s_w.wait(te, 16 * (40 * sp + 40))
                s_te.wait(te, 64 * (sp + 1))
                for idx in range(32):
                    hc, tt2 = idx // 2, idx % 2
                    par = idx % 2
                    gidx = 32 * sp + idx
                    if gidx >= 2:
                        s_fc.wait(te, gidx - 1)
                    last = None
                    for ic in range(8):
                        last = te.matmul(psD[par][:, :], WD[:, ic, 128 * hc:128 * (hc + 1)],
                                         IT[:, ic, 512 * tt2:512 * (tt2 + 1)],
                                         start=(ic == 0), stop=(ic == 7))
                    s_phB.inc(last, 1)

        @block.scalar
        def _(sc):
            for sp in range(NSUP):
                if "A" not in parts:
                    continue
                for idx in range(16):
                    par = idx % 2
                    gidx = 16 * sp + idx
                    s_phA.wait(sc, gidx + 1)
                    if gidx >= 2:
                        s_h.wait(sc, gidx - 1)
                    sc.activation(sgt[par][:, :], psG[par][:, :], AF.Silu)
                    s_ev.inc(sc.activation(uft[par][:, :], psU[par][:, :], AF.Identity), 1)
                if "T" in parts:
                    for j in range(64):
                        tc, ic = j // 8, j % 8
                        gj = 64 * sp + j
                        if j == 0 and sp >= 1 and "B" in parts:
                            s_phB.wait(sc, 32 * sp)
                        s_tr.wait(sc, gj + 1)
                        s_te.inc(sc.activation(IT[:, ic, 128 * tc:128 * (tc + 1)],
                                               psT[j % 2][:, :], AF.Identity), 1)
                if "B" in parts:
                    for idx in range(32):
                        par = idx % 2
                        gidx = 32 * sp + idx
                        s_phB.wait(sc, gidx + 1)
                        if gidx >= 4:
                            s_fd.wait(sc, 16 * (gidx - 3))
                        s_fc.inc(sc.activation(stg[gidx % 4][:, :], psD[par][:, :],
                                               AF.Identity, scale=0.5), 1)

        @block.vector
        def _(v):
            for sp in range(NSUP):
                if "A" not in parts:
                    continue
                for idx in range(16):
                    tc, it = idx // 2, idx % 2
                    par = idx % 2
                    gidx = 16 * sp + idx
                    s_ev.wait(v, gidx + 1)
                    if idx == 0 and sp >= 1 and "T" in parts:
                        s_tr.wait(v, 64 * sp)
                    v.tensor_tensor(ht[par][:, :], sgt[par][:, :], uft[par][:, :], ALU.mult)
                    if "S" not in parts:
                        s_h.inc(v.tensor_copy(HS[:, tc, 512 * it:512 * (it + 1)], ht[par][:, :]), 1)
                        continue
                    v.tensor_tensor(abt[par][:, :], ht[par][:, :], ht[par][:, :], ALU.mult)
                    A = abt[par][:, :].rearrange("p (g f) -> p g f", f=4)
                    M1 = mm1[par][:, :].rearrange("p (g o) -> p g o", o=1)
                    M2 = mm2[par][:, :].rearrange("p (g o) -> p g o", o=1)
                    v.tensor_tensor(M1, A[:, :, 0:1], A[:, :, 1:2], ALU.max)
                    v.tensor_tensor(M2, A[:, :, 2:3], A[:, :, 3:4], ALU.max)
                    v.tensor_tensor(t2t[par][:, :], mm1[par][:, :], mm2[par][:, :], ALU.min)
                    v.tensor_tensor(M1, A[:, :, 0:1], A[:, :, 1:2], ALU.min)
                    v.tensor_tensor(M2, A[:, :, 2:3], A[:, :, 3:4], ALU.min)
                    v.tensor_tensor(mm1[par][:, :], mm1[par][:, :], mm2[par][:, :], ALU.max)
                    v.tensor_tensor(t2t[par][:, :], t2t[par][:, :], mm1[par][:, :], ALU.max)
                    v.tensor_tensor(get_[par][:, :].rearrange("p (g f) -> p g f", f=4),
                                    A,
                                    t2t[par][:, :].rearrange("p (g o) -> p g o", o=1)
                                    .broadcast_to((128, 128, 4)),
                                    ALU.is_ge)
                    s_h.inc(v.tensor_tensor(HS[:, tc, 512 * it:512 * (it + 1)],
                                            ht[par][:, :], get_[par][:, :], ALU.mult), 1)
    ctx.barrier()



def emit_xn_gather(ctx, io, xn, contrib, gathered):
    block = ctx.block
    s_ct = ctx.sem("xgct")
    s_ag = ctx.gsem("xgag")

    @block.gpsimd
    def _(g):
        for c in range(NCH):
            s_ct.dinc(g.dma_start(out=contrib[c, :, :], in_=xn[:, c, :]))
        s_ct.wait_abs(g)
        s_ag.inc(g.collective_compute(
            "AllGather", ALU.bypass, replica_groups=[list(range(NCORE))],
            ins=[contrib.ap().opt()], outs=[gathered.ap().opt()]), 1)
        s_ag.wait(g, 1)
    ctx.barrier()



def emit_attention(ctx, io, xs_out):
    nc, block = ctx.nc, ctx.block
    NB = 4
    Kd = ctx.sb("at_kd", [128, NB * S], BF16)
    Vg = ctx.sb("at_vg", [128, NB * 32 * 65], BF16)
    WQ = ctx.sb("at_wq", [128, NCH, 256], BF16)
    WKV = ctx.sb("at_wkv", [128, NCH, 128], BF16)
    WO = ctx.sb("at_wo", [128, 2, D], BF16)
    PSW = ctx.sb("at_pswt", [128, 128], F32)
    RC = ctx.sb("at_rc", [128, S], F32)
    RS_ = ctx.sb("at_rst", [128, S], F32)
    SLP = ctx.sb("at_slp", [128, 4], F32)
    BOT = ctx.sb("at_bo", [128, NCH], F32)

    s_ld = ctx.sem("atld")
    s_ct = ctx.sem("atct")
    s_ag = ctx.sem("atag")

    @block.sync
    def _(sync):
        s_ld.dinc(sync.dma_start(out=PSW[:, :], in_=io["pswapf"][:, :]))
        s_ld.dinc(sync.dma_start(out=RC[:, :], in_=io["rc"][:, :]))
        s_ld.dinc(sync.dma_start(out=RS_[:, :], in_=io["rs"][:, :]))
        s_ld.dinc(sync.dma_start(out=SLP[:, :], in_=io["slopes"][:, :]))
        s_ld.dinc(sync.dma_start(out=BOT[:, :], in_=io["bo"][:, :]))

    @block.gpsimd
    def _(g):
        for kc in range(NCH):
            s_ld.dinc(g.dma_start(out=WQ[:, kc, :], in_=io["wq"][128 * kc:128 * (kc + 1), :]))
            s_ld.dinc(g.dma_start(out=WKV[:, kc, :], in_=io["wkv"][128 * kc:128 * (kc + 1), :]))
        for cc in range(2):
            s_ld.dinc(g.dma_start(out=WO[:, cc, :], in_=io["wo"][128 * cc:128 * (cc + 1), :]))
        g.memset(Vg[:, :], 1.0)
    ctx.barrier()

    with contextlib.ExitStack() as lst:
        xgt = [ctx.sb("at_xgt", [128, NCH, 512], BF16, lst) for _ in range(2)]
        QE = [[ctx.sb("at_qe", [128, 512], F32, lst) for _ in range(2)] for _ in range(2)]
        KE = [ctx.sb("at_ke", [64, 512], F32, lst) for _ in range(2)]
        VE = [ctx.sb("at_ve", [64, 512], BF16, lst) for _ in range(2)]
        T1 = ctx.sb("at_t1", [128, 512], F32, lst)
        T2 = ctx.sb("at_t2", [128, 512], F32, lst)
        QR = [ctx.sb("at_qr", [128, 512], BF16, lst) for _ in range(2)]
        psQ = [ctx.ps("at_pq", [128, 512], F32, lst) for _ in range(2)]
        psKV = ctx.ps("at_pkv", [128, 512], F32, lst)
        psSw = [ctx.ps("at_psw", [128, 512], F32, lst) for _ in range(2)]
        psSwK = ctx.ps("at_pswk", [64, 512], F32, lst)
        psT = [ctx.ps("at_pt", [128, 64], BF16, lst) for _ in range(2)]

        s_xg = ctx.sem("atxg")
        s_mm = ctx.sem("atmm")
        s_qe = ctx.sem("atqe")
        s_sw = ctx.sem("atsw")
        s_rp = ctx.sem("atrp")
        s_qd = ctx.sem("atqd")
        s_vte = ctx.sem("atvte")
        s_kdup = ctx.sem("atkdup")

        NIT = 32
        for i in range(NIT):
            s8, tch = i // 4, i % 4
            b, hf = s8 % 4, s8 // 4
            par = i % 2
            bcol = b * S + hf * T + 512 * tch
            tloc = hf * T + 512 * tch
            ib = b * 32 + hf * 16 + tch * 4

            @block.sync
            def _(sync, i=i, par=par, s8=s8, tch=tch):
                if i >= 2:
                    s_mm.wait(sync, 3 * (i - 1))
                for kc in range(NCH):
                    s_xg.dinc(sync.dma_start(out=xgt[par][:, kc, :],
                                             in_=io["xg"][s8, kc, :, 512 * tch:512 * (tch + 1)]))

            @block.tensor
            def _(te, i=i, par=par):
                s_xg.wait(te, 256 * (i + 1))
                if i == 0:
                    s_ld.wait_abs(te)
                if i >= 1:
                    s_qe.wait(te, 4 * i)
                for qc in range(2):
                    last = None
                    for kc in range(NCH):
                        last = te.matmul(psQ[qc][:, :], WQ[:, kc, 128 * qc:128 * (qc + 1)],
                                         xgt[par][:, kc, :], start=(kc == 0), stop=(kc == NCH - 1))
                    s_mm.inc(last, 1)
                last = None
                for kc in range(NCH):
                    last = te.matmul(psKV[:, :], WKV[:, kc, :], xgt[par][:, kc, :],
                                     start=(kc == 0), stop=(kc == NCH - 1))
                s_mm.inc(last, 1)
                s_qe.wait(te, 4 * (i + 1))
                if i >= 1:
                    s_rp.wait(te, 3 * i)
                s_sw.inc(te.matmul(psSw[0][:, :], PSW[:, :], QE[par][0][:, :],
                                   start=True, stop=True), 1)
                s_sw.inc(te.matmul(psSw[1][:, :], PSW[:, :], QE[par][1][:, :],
                                   start=True, stop=True), 1)
                s_sw.inc(te.matmul(psSwK[:, :], PSW[0:64, 0:64], KE[par][:, :],
                                   start=True, stop=True), 1)
                for j in range(4):
                    if 4 * i + j >= 2:
                        s_vte.wait(te, 4 * i + j - 1)
                    s_sw.inc(te.transpose(psT[j % 2][:, :], VE[par][:, 128 * j:128 * (j + 1)],
                                          io["ident_t"][0:64, 0:64]), 1)

            @block.scalar
            def _(sc, i=i, par=par, ib=ib):
                s_mm.wait(sc, 3 * (i + 1))
                if i >= 2:
                    s_rp.wait(sc, 3 * (i - 1))
                    s_sw.wait(sc, 7 * (i - 1))
                sc.activation(QE[par][0][:, :], psQ[0][:, :], AF.Identity)
                sc.activation(QE[par][1][:, :], psQ[1][:, :], AF.Identity)
                sc.activation(KE[par][:, :], psKV[0:64, :], AF.Identity)
                s_qe.inc(sc.activation(VE[par][:, :], psKV[64:128, :], AF.Identity), 4)
                for j in range(4):
                    s_sw.wait(sc, 7 * i + 4 + j)
                    s_vte.inc(sc.activation(Vg[:, 65 * (ib + j):65 * (ib + j) + 64],
                                            psT[j % 2][:, :], AF.Identity), 1)

            @block.vector
            def _(v, i=i, par=par, bcol=bcol, tloc=tloc):
                s_sw.wait(v, 7 * i + 3)
                if i >= 1:
                    s_qd.wait(v, 32 * i)
                for qc in range(2):
                    v.tensor_tensor(T1[:, :], QE[par][qc][:, :], RC[:, tloc:tloc + 512], ALU.mult)
                    v.tensor_tensor(T2[:, :], psSw[qc][:, :], RS_[:, tloc:tloc + 512], ALU.mult)
                    s_rp.inc(v.tensor_tensor(QR[qc][:, :], T1[:, :], T2[:, :], ALU.add), 1)
                v.tensor_tensor(T1[0:64, :], KE[par][:, :], RC[0:64, tloc:tloc + 512], ALU.mult)
                v.tensor_tensor(T2[0:64, :], psSwK[:, :], RS_[0:64, tloc:tloc + 512], ALU.mult)
                s_rp.inc(v.tensor_tensor(Kd[0:64, bcol:bcol + 512], T1[0:64, :],
                                         T2[0:64, :], ALU.add), 1)

            @block.sync
            def _(sync, i=i, s8=s8, tch=tch):
                b, hf = s8 % 4, s8 // 4
                s_rp.wait(sync, 3 * i + 2)
                for qc in range(2):
                    s_qd.dinc(sync.dma_start(
                        out=io["Qd"][b, 128 * qc:128 * (qc + 1),
                                     hf * T + 512 * tch:hf * T + 512 * (tch + 1)],
                        in_=QR[qc][:, :]))

            @block.gpsimd
            def _(g, i=i, bcol=bcol):
                s_rp.wait(g, 3 * (i + 1))
                s_kdup.inc(g.tensor_copy(Kd[64:128, bcol:bcol + 512],
                                         Kd[0:64, bcol:bcol + 512]), 1)

        @block.sync
        def _(sync):
            s_qd.wait_abs(sync)

        @block.gpsimd
        def _(g):
            s_kdup.wait_abs(g)
        ctx.barrier()

    late_stack = contextlib.ExitStack()
    CTX = ctx.sb("at_ctxv", [128, 2, NB * S], BF16, late_stack)
    with contextlib.ExitStack() as lst:
        qt = [ctx.sb("at_qt", [128, 512], BF16, lst) for _ in range(4)]
        dif = [ctx.sb("at_dif", [128, 512], F32, lst) for _ in range(2)]
        alit = [ctx.sb("at_ali", [128, 512], F32, lst) for _ in range(2)]
        msk = [ctx.sb("at_msk", [128, 512], BF16, lst) for _ in range(2)]
        s1t = [ctx.sb("at_s1", [128, 512], F32, lst) for _ in range(4)]
        tt_ = [ctx.sb("at_tt", [128, 512], F32, lst) for _ in range(2)]
        prb = [ctx.sb("at_prb", [128, 512], BF16, lst) for _ in range(4)]
        ce = [ctx.sb("at_ce", [128, 512], F32, lst) for _ in range(2)]
        srow = ctx.sb("at_srow", [1, 512], F32, lst)
        r16 = ctx.sb("at_r16", [1, 512], BF16, lst)
        psS = [ctx.ps("at_pss", [128, 512], F32, lst) for _ in range(4)]
        psC = [ctx.ps("at_pc", [128, 512], F32, lst) for _ in range(4)]

        s_q = ctx.sem("atq")
        s_sc = ctx.sem("atsc")
        s_s1 = ctx.sem("ats1")
        s_ex = ctx.sem("atex")
        s_mk = ctx.sem("atmk")
        s_av = ctx.sem("atav")
        s_dg = ctx.sem("atdg")
        s_al = ctx.sem("atal")
        s_ce = ctx.sem("atce")
        s_rc2 = ctx.sem("atrc2")
        s_r16 = ctx.sem("atr16")
        s_bc = ctx.sem("atbc")
        s_cnw = ctx.sem("atcnw")

        ht_c = 0
        tg_c = 0
        dg_c = 0
        it_c = 0
        meta = []
        for b in range(NB):
            for qch in range(8):
                nkt = 4 * qch + 4
                q0 = 512 * qch
                base_ht = ht_c
                base_tg = tg_c
                base_it = it_c

                @block.sync
                def _(sync, b=b, qch=qch, base_ht=base_ht):
                    if base_ht > 0:
                        s_sc.wait(sync, base_ht)
                    for h in range(4):
                        for half in range(2):
                            s_q.dinc(sync.dma_start(
                                out=qt[h][64 * half:64 * (half + 1), :],
                                in_=io["Qd"][b, 64 * h:64 * (h + 1), 512 * qch:512 * (qch + 1)]))

                @block.gpsimd
                def _(g, b=b, qch=qch, nkt=nkt, q0=q0, base_tg=base_tg, base_ht=base_ht,
                      base_dg=dg_c):
                    mkc = base_dg
                    for ktj in range(nkt):
                        tg = base_tg + ktj
                        diag = 128 * (ktj + 1) > q0
                        if tg >= 2:
                            s_al.wait(g, tg - 1)
                        g.iota(dif[tg % 2][:, :], [[1, 512]], base=q0 - 128 * ktj,
                               channel_multiplier=-1, allow_small_or_imprecise_dtypes=True)
                        if diag:
                            g.tensor_scalar(msk[tg % 2][:, :], dif[tg % 2][:, :], 0.0,
                                            None, ALU.is_ge)
                        s_dg.inc(g.memset(ctx.scr[:, 4:5], 0.0), 1)
                        if diag:
                            for h in range(4):
                                ht = base_ht + 4 * ktj + h
                                s_ex.wait(g, ht + 1)
                                mkc += 1
                                s_mk.inc(g.tensor_tensor(prb[ht % 4][:, :], prb[ht % 4][:, :],
                                                         msk[tg % 2][:, :], ALU.mult), 1)

                @block.scalar
                def _(sc, nkt=nkt, q0=q0, base_tg=base_tg, base_ht=base_ht):
                    for ktj in range(nkt):
                        tg = base_tg + ktj
                        s_dg.wait(sc, tg + 1)
                        s_al.inc(sc.activation(alit[tg % 2][:, :], dif[tg % 2][:, :], AF.Abs), 1)
                        for h in range(4):
                            ht = base_ht + 4 * ktj + h
                            s_s1.wait(sc, ht + 1)
                            if ht >= 4:
                                s_av.wait(sc, ht - 3)
                            sc.activation(tt_[ht % 2][:, :], s1t[ht % 4][:, :], AF.Tanh,
                                          scale=1.0 / 30.0)
                            s_ex.inc(sc.activation(prb[ht % 4][:, :], tt_[ht % 2][:, :],
                                                   AF.Exp, scale=30.0), 1)

                @block.vector
                def _(v, nkt=nkt, base_tg=base_tg, base_ht=base_ht):
                    for ktj in range(nkt):
                        tg = base_tg + ktj
                        s_al.wait(v, tg + 1)
                        for h in range(4):
                            ht = base_ht + 4 * ktj + h
                            s_sc.wait(v, ht + 1)
                            if ht >= 4:
                                s_ex.wait(v, ht - 3)
                            s_s1.inc(v.scalar_tensor_tensor(
                                s1t[ht % 4][:, :], alit[tg % 2][:, :], SLP[:, h:h + 1],
                                psS[ht % 4][:, :], ALU.mult, ALU.add), 1)

                @block.tensor
                def _(te, b=b, qch=qch, nkt=nkt, base_tg=base_tg, base_ht=base_ht,
                      base_it=base_it, meta=meta):
                    if base_ht == 0:
                        s_ld.wait_abs(te)
                    def emit_av(ht2):
                        is_diag, mkcum, h2, ib2, first, last_ = meta[ht2]
                        if is_diag:
                            s_mk.wait(te, mkcum)
                        else:
                            s_ex.wait(te, ht2 + 1)
                        m2 = ht2 % 4
                        ins = te.matmul(psC[h2][0:65, :], Vg[:, 65 * ib2:65 * (ib2 + 1)],
                                        prb[m2][:, :], start=first, stop=last_)
                        s_av.inc(ins, 1)
                    for ktj in range(nkt):
                        tg = base_tg + ktj
                        for h in range(4):
                            ht = base_ht + 4 * ktj + h
                            diag = 128 * (ktj + 1) > 512 * qch
                            mkcum = (meta[-1][1] if meta else 0) + (1 if diag else 0)
                            meta.append((diag, mkcum, h, b * 32 + ktj, ktj == 0, ktj == nkt - 1))
                            s_q.wait(te, 128 * (base_it + 1))
                            if ht >= 4:
                                s_s1.wait(te, ht - 3)
                            if ktj == 0:
                                s_ce.wait(te, 2 * 4 * base_it)
                                s_cnw.wait(te, 4 * base_it)
                            s_sc.inc(te.matmul(psS[ht % 4][:, :],
                                               Kd[0:64, b * S + 128 * ktj:b * S + 128 * (ktj + 1)],
                                               qt[h][0:64, :], start=True, stop=True), 1)
                            if ht >= 4:
                                emit_av(ht - 4)
                    for ht2 in range(base_ht + 4 * nkt - 4, base_ht + 4 * nkt):
                        emit_av(ht2)
                    for h in range(4):
                        s_r16.wait(te, 4 * base_it + h + 1)
                        if base_it > 0 or h >= 1:
                            s_cnw.wait(te, 4 * base_it + h)
                        s_bc.inc(te.matmul(psS[0][0:64, :], io["ones_row_t"][:, 0:64],
                                           r16[:, :], start=True, stop=True), 1)

                @block.scalar
                def _(sc, base_ht=base_ht, nkt=nkt, base_it=base_it):
                    for h in range(4):
                        s_av.wait(sc, base_ht + 4 * nkt)
                        if base_it > 0 or h >= 2:
                            s_cnw.wait(sc, 4 * base_it + h - 1)
                        sc.activation(ce[h % 2][0:65, :], psC[h][0:65, :], AF.Identity)
                        s_ce.inc(sc.activation(srow[:, :], ce[h % 2][64:65, :], AF.Identity), 2)
                        s_rc2.wait(sc, 4 * base_it + h + 1)
                        if base_it > 0 or h >= 1:
                            s_bc.wait(sc, 4 * base_it + h)
                        s_r16.inc(sc.activation(r16[:, :], srow[:, :], AF.Identity), 1)

                @block.vector
                def _(v, b=b, qch=qch, base_it=base_it):
                    for h in range(4):
                        s_ce.wait(v, 2 * (4 * base_it + h + 1))
                        s_rc2.inc(v.reciprocal(srow[:, :], srow[:, :]), 1)
                        s_bc.wait(v, 4 * base_it + h + 1)
                        s_cnw.inc(v.tensor_tensor(
                            CTX[64 * (h % 2):64 * (h % 2) + 64, h // 2,
                                b * S + 512 * qch:b * S + 512 * (qch + 1)],
                            ce[h % 2][0:64, :], psS[0][0:64, :], ALU.mult), 1)

                ht_c += 4 * nkt
                tg_c += nkt
                dg_c += 16
                it_c += 1
        ctx.barrier()

        if "ctx_o" in io:
            s_do = ctx.sem("atdo")

            @block.gpsimd
            def _(g):
                for cc in range(2):
                    s_do.dinc(g.dma_start(out=io["ctx_o"][:, cc, :], in_=CTX[:, cc, :]))
                s_do.wait_abs(g)
            ctx.barrier()

    with contextlib.ExitStack() as lst:
        pso = [ctx.ps("at_po", [128, 512], F32, lst) for _ in range(2)]
        ost = [ctx.sb("at_ost", [128, 512], BF16, lst) for _ in range(4)]
        rch = [ctx.sb("at_rch", [128, T], BF16, lst) for _ in range(2)]
        stg2 = [ctx.sb("at_stg2", [128, T], F32, lst) for _ in range(2)]
        s_om = ctx.sem("atom")
        s_oe = ctx.sem("atoe")
        s_od = ctx.sem("atod")
        s_rs = ctx.gsem("atrsc")
        s_rl = ctx.sem("atrl")
        s_pa = ctx.sem("atpa")

        @block.tensor
        def _(te):
            k = 0
            for s8 in range(8):
                for dc in range(NCH):
                    for tt2 in range(4):
                        if k >= 2:
                            s_oe.wait(te, k - 1)
                        col0 = (s8 % 4) * S + (s8 // 4) * T + 512 * tt2
                        last = None
                        for cc in range(2):
                            last = te.matmul(pso[k % 2][:, :], WO[:, cc, 128 * dc:128 * (dc + 1)],
                                             CTX[:, cc, col0:col0 + 512],
                                             start=(cc == 0), stop=(cc == 1))
                        s_om.inc(last, 1)
                        k += 1

        @block.scalar
        def _(sc):
            for k in range(8 * NCH * 4):
                s_om.wait(sc, k + 1)
                if k >= 4:
                    s_od.wait(sc, 16 * (k - 3))
                s_oe.inc(sc.activation(ost[k % 4][:, :], pso[k % 2][:, :], AF.Identity), 1)

        @block.sync
        def _(sync):
            k = 0
            for s8 in range(8):
                for dc in range(NCH):
                    for tt2 in range(4):
                        s_oe.wait(sync, k + 1)
                        s_od.dinc(sync.dma_start(
                            out=io["opart"][s8, dc, :, 512 * tt2:512 * (tt2 + 1)],
                            in_=ost[k % 4][:, :]))
                        k += 1
            s_od.wait_abs(sync)

        @block.scalar
        def _(sc):
            base = 8 * NCH * 4
            for c in range(NCH):
                s_rl.wait(sc, 16 * (c + 1))
                if c >= 2:
                    s_pa.wait(sc, 16 * (c - 1))
                s_oe.inc(sc.activation(stg2[c % 2][:, :], rch[c % 2][:, :], AF.Identity,
                                       bias=BOT[:, c:c + 1]), 1)

        @block.gpsimd
        def _(g):
            base = 8 * NCH * 4
            s_od.wait(g, 16 * 8 * NCH * 4)
            s_rs.inc(g.collective_compute(
                "ReduceScatter", ALU.add, replica_groups=[list(range(NCORE))],
                ins=[io["opart"].ap().opt()], outs=[io["rsout"].ap().opt()]), 1)
            s_rs.wait(g, 1)
            for c in range(NCH):
                s_rl.dinc(g.dma_start(out=rch[c % 2][:, :], in_=io["rsout"][c, :, :]))
                if c >= 1:
                    s_oe.wait(g, base + c)
                    s_pa.dinc(g.dma_start(out=xs_out[c - 1, :, :], in_=stg2[(c - 1) % 2][:, :],
                                          accum_op=ALU.add))
            s_oe.wait(g, base + NCH)
            s_pa.dinc(g.dma_start(out=xs_out[NCH - 1, :, :], in_=stg2[(NCH - 1) % 2][:, :],
                                  accum_op=ALU.add))
            s_pa.wait_abs(g)
    ctx.barrier()
    late_stack.close()



def emit_ssm(ctx, io, xs_out):
    nc, block = ctx.nc, ctx.block
    NB = 4
    WIN = ctx.sb("sm_win", [128, NCH, 96], BF16)
    WDT = ctx.sb("sm_wdt", [64, 256], BF16)
    BDT = ctx.sb("sm_bdt", [128, 2], F32)
    AEXP = ctx.sb("sm_aexp", [128, 32], F32)
    DPT = ctx.sb("sm_dpt", [128, 2], F32)
    IDXB = ctx.sb("sm_idxb", [128, 1], I32)
    OH = ctx.sb("sm_oh", [32, 32 * 128], BF16)
    BC16 = ctx.sb("sm_bc16", [32, 8 * T], BF16)

    s_ld = ctx.sem("smld")

    @block.gpsimd
    def _(g):
        for kc in range(NCH):
            s_ld.dinc(g.dma_start(out=WIN[:, kc, :], in_=io["winT"][128 * kc:128 * (kc + 1), :]))
        s_ld.dinc(g.dma_start(out=WDT[:, :], in_=io["wdtT"][:, :]))

    @block.sync
    def _(sync):
        s_ld.dinc(sync.dma_start(out=BDT[:, :], in_=io["bdt"][:, :]))
        s_ld.dinc(sync.dma_start(out=AEXP[:, :], in_=io["alogT"][:, :]))
        s_ld.dinc(sync.dma_start(out=DPT[:, :], in_=io["Dp"][:, :]))
        s_ld.dinc(sync.dma_start(out=IDXB[:, :], in_=io["idxbase"][:, :]))
        s_ld.dinc(sync.dma_start(out=OH[:, :], in_=io["onehot32"][:, :]))

    @block.scalar
    def _(sc):
        s_ld.wait_abs(sc)
        sc.activation(AEXP[:, :], AEXP[:, :], AF.Exp)
        sc.activation(AEXP[:, :], AEXP[:, :], AF.Identity, scale=-1.0)
    ctx.barrier()

    with contextlib.ExitStack() as lst:
        xgt = ctx.sb("sm_xgt", [128, NCH, 512], BF16, lst)
        DLRS = ctx.sb("sm_dlrs", [64, 512], BF16, lst)
        psD_ = [ctx.ps("sm_psd", [128, 512], F32, lst) for _ in range(2)]
        s_xg = ctx.sem("smxg")
        s_mm = ctx.sem("smmm")
        s_ev = ctx.sem("smev")
        s_dw = ctx.sem("smdw")

        for i in range(32):
            s8, tt = i // 4, i % 4

            @block.sync
            def _(sync, i=i, s8=s8, tt=tt):
                if i >= 1:
                    s_mm.wait(sync, i)
                for kc in range(NCH):
                    r0 = 2048 * s8 + 128 * kc
                    s_xg.dinc(sync.dma_start(out=xgt[:, kc, :],
                                             in_=io["xg2f"][r0:r0 + 128, 512 * tt:512 * (tt + 1)]))

            @block.tensor
            def _(te, i=i):
                s_xg.wait(te, 256 * (i + 1))
                if i == 0:
                    s_ld.wait_abs(te)
                if i >= 2:
                    s_ev.wait(te, 2 * (i - 1))
                last = None
                for kc in range(NCH):
                    last = te.matmul(psD_[i % 2][0:96, :], WIN[:, kc, :], xgt[:, kc, :],
                                     start=(kc == 0), stop=(kc == NCH - 1))
                s_mm.inc(last, 1)

            @block.scalar
            def _(sc, i=i, s8=s8, tt=tt):
                s_mm.wait(sc, i + 1)
                if i >= 1:
                    s_dw.wait(sc, 16 * i)
                col = 2048 * s8 + 512 * tt
                sc.activation(DLRS[:, :], psD_[i % 2][0:64, :], AF.Identity)
                s_ev.inc(sc.activation(BC16[:, col:col + 512], psD_[i % 2][64:96, :],
                                       AF.Identity), 2)

            @block.gpsimd
            def _(g, i=i, s8=s8, tt=tt):
                s_ev.wait(g, 2 * (i + 1))
                col = 2048 * s8 + 512 * tt
                s_dw.dinc(g.dma_start(out=io["dlr_dram"][:, col:col + 512], in_=DLRS[:, :]))

        @block.gpsimd
        def _(g):
            s_dw.wait_abs(g)
        ctx.barrier()

    with contextlib.ExitStack() as lst:
        xs16 = ctx.sb("sm_xs", [128, 2, 2 * T], BF16, lst)
        d16 = ctx.sb("sm_d16", [128, 2, 2 * T], BF16, lst)
        DLRB = ctx.sb("sm_dlrb", [64, 2 * T], BF16, lst)
        tmp16 = [ctx.sb("sm_tmp", [128, 2 * T], BF16, lst) for _ in range(2)]
        bb16 = [ctx.sb("sm_bb", [128, 2 * T], BF16, lst) for _ in range(2)]
        h16 = [ctx.sb("sm_h", [128, 2 * T], BF16, lst) for _ in range(2)]
        yt16 = [ctx.sb("sm_yt", [128, 2 * T], BF16, lst) for _ in range(2)]
        yb = ctx.sb("sm_yb", [128, 2, 2 * T], BF16, lst)
        Bmb = [ctx.sb("sm_bmb", [128, 2 * T], BF16, lst) for _ in range(2)]
        Cmb = [ctx.sb("sm_cmb", [128, 2 * T], BF16, lst) for _ in range(2)]
        idxu = ctx.sb("sm_idxu", [128, 4], U32, lst)
        hcol = [ctx.sb("sm_hcol", [128, 1], F32, lst) for _ in range(2)]
        io_dbg_h_tile = None
        Xsp = ctx.sb("sm_xsp", [128, 512], F32, lst)
        Rsp = ctx.sb("sm_rsp", [128, 512], F32, lst)
        Asp = ctx.sb("sm_asp", [128, 512], F32, lst)
        psd2 = [ctx.ps("sm_psd2", [128, 512], F32, lst) for _ in range(2)]
        psb = [ctx.ps("sm_psb", [128, 512], F32, lst) for _ in range(2)]

        s_idx = ctx.sem("smidx")
        s_gx = ctx.sem("smgx")
        s_dl = ctx.sem("smdl")
        s_dm = ctx.sem("smdm")
        s_de = ctx.sem("smde")
        s_dc = ctx.sem("smdc")
        s_dx = ctx.sem("smdx")
        s_bm = ctx.sem("smbm")
        s_be = ctx.sem("smbe")
        s_tm = ctx.sem("smtm")
        s_ae = ctx.sem("smae")
        s_bb = ctx.sem("smbb")
        s_sn = ctx.sem("smsn")
        s_h1 = ctx.sem("smh1")
        s_h2 = ctx.sem("smh2")
        s_yt = ctx.sem("smyt")
        s_ya = ctx.sem("smya")
        s_yw = ctx.sem("smyw")

        for b in range(NB):
            @block.vector
            def _(v, b=b):
                if b >= 1:
                    s_gx.wait(v, 64 * b)
                for cc2 in range(2):
                    for hf in range(2):
                        s_idx.inc(v.tensor_scalar(idxu[:, 2 * cc2 + hf:2 * cc2 + hf + 1],
                                                  IDXB[:, :], 2048 * (b + 4 * hf) + 128 * cc2,
                                                  None, ALU.add), 1)

            @block.gpsimd
            def _(g, b=b):
                for cc2 in range(2):
                    for hf in range(2):
                        s_idx.wait(g, 4 * b + 2 * cc2 + hf + 1)
                        if b >= 1:
                            s_dx.wait(g, 4 * b)
                            s_bb.wait(g, 32 * b)
                        s_gx.dinc(g.indirect_dma_start(
                            out=xs16[:, cc2, T * hf:T * (hf + 1)], out_offset=None,
                            in_=io["xg2f"].ap(),
                            in_offset=IndirectOffsetOnAxis(
                                ap=idxu[:, 2 * cc2 + hf:2 * cc2 + hf + 1], axis=0)))

            @block.sync
            def _(sync, b=b):
                if b >= 1:
                    s_dm.wait(sync, 16 * b)
                for hf in range(2):
                    s_dl.dinc(sync.dma_start(
                        out=DLRB[:, T * hf:T * (hf + 1)],
                        in_=io["dlr_dram"][:, 2048 * (b + 4 * hf):2048 * (b + 4 * hf + 1)]))

            @block.tensor
            def _(te, b=b):
                s_dl.wait(te, 32 * (b + 1))
                for cc2 in range(2):
                    for ct8 in range(8):
                        k = 16 * b + 8 * cc2 + ct8
                        if k >= 2:
                            s_de.wait(te, k - 1)
                        s_dm.inc(te.matmul(
                            psd2[k % 2][:, :], WDT[:, 128 * cc2:128 * (cc2 + 1)],
                            DLRB[:, 512 * ct8:512 * (ct8 + 1)], start=True, stop=True), 1)

            @block.scalar
            def _(sc, b=b):
                for cc2 in range(2):
                    for ct8 in range(8):
                        k = 16 * b + 8 * cc2 + ct8
                        s_dm.wait(sc, k + 1)
                        if k >= 1:
                            s_dc.wait(sc, k)
                        sc.activation(Xsp[:, :], psd2[k % 2][:, :], AF.Identity,
                                      bias=BDT[:, cc2:cc2 + 1])
                        sc.activation(Rsp[:, :], Xsp[:, :], AF.Relu)
                        sc.activation(Asp[:, :], Xsp[:, :], AF.Abs)
                        sc.activation(Asp[:, :], Asp[:, :], AF.Exp, scale=-1.0)
                        s_de.inc(sc.activation(Asp[:, :], Asp[:, :], AF.Ln, bias=1.0), 1)

            @block.vector
            def _(v, b=b):
                for cc2 in range(2):
                    for ct8 in range(8):
                        k = 16 * b + 8 * cc2 + ct8
                        s_de.wait(v, k + 1)
                        hf, ct = ct8 // 4, ct8 % 4
                        s_dc.inc(v.tensor_tensor(
                            d16[:, cc2, T * hf + 512 * ct:T * hf + 512 * (ct + 1)],
                            Rsp[:, :], Asp[:, :], ALU.add), 1)
                    s_gx.wait(v, 16 * (4 * b + 4))
                    if b >= 1 and cc2 == 0:
                        s_yw.wait(v, 64 * b)
                    v.tensor_scalar(yb[:, cc2, :], xs16[:, cc2, :],
                                    DPT[:, cc2:cc2 + 1], None, ALU.mult)
                    s_dx.inc(v.tensor_tensor(xs16[:, cc2, :], d16[:, cc2, :],
                                             xs16[:, cc2, :], ALU.mult), 2)

            if b == 0 and "dbg_d16" in io:
                s_dbg = ctx.sem("smdbg")

                @block.gpsimd
                def _(g):
                    s_dx.wait(g, 4)
                    for cc2 in range(2):
                        s_dbg.dinc(g.dma_start(out=io["dbg_d16"][:, cc2, :], in_=d16[:, cc2, :]))
                        s_dbg.dinc(g.dma_start(out=io["dbg_dx"][:, cc2, :], in_=xs16[:, cc2, :]))
                        s_dbg.dinc(g.dma_start(out=io["dbg_xs"][:, cc2, :], in_=xs16[:, cc2, :]))
                    s_dbg.wait_abs(g)

            for n in range(16):
                m = b * 16 + n

                @block.tensor
                def _(te, b=b, n=n, m=m):
                    for kind in range(2):
                        row = n if kind == 0 else 16 + n
                        for sg in range(8):
                            kk = 16 * m + 8 * kind + sg
                            if kk >= 2:
                                s_be.wait(te, kk - 1)
                            hf, sg4 = sg // 4, sg % 4
                            col = 2048 * (b + 4 * hf) + 512 * sg4
                            s_bm.inc(te.matmul(
                                psb[kk % 2][:, :], OH[:, 128 * row:128 * (row + 1)],
                                BC16[:, col:col + 512], start=True, stop=True), 1)

                @block.scalar
                def _(sc, b=b, n=n, m=m):
                    for cc2e in range(2):
                        je = b * 32 + n * 2 + cc2e
                        s_dx.wait(sc, 4 * b + 2 * (cc2e + 1))
                        if je >= 2:
                            s_sn.wait(sc, je - 1)
                        s_ae.inc(sc.activation(tmp16[je % 2][:, :], d16[:, cc2e, :], AF.Exp,
                                               scale=AEXP[:, 16 * cc2e + n:16 * cc2e + n + 1]), 1)
                    for kind in range(2):
                        dst = Bmb[m % 2] if kind == 0 else Cmb[m % 2]
                        for sg in range(8):
                            kk = 16 * m + 8 * kind + sg
                            s_bm.wait(sc, kk + 1)
                            if m >= 2 and sg == 0:
                                if kind == 0:
                                    s_bb.wait(sc, 2 * m - 2)
                                else:
                                    s_yt.wait(sc, 2 * m - 2)
                            hf, sg4 = sg // 4, sg % 4
                            s_be.inc(sc.activation(dst[:, T * hf + 512 * sg4:T * hf + 512 * (sg4 + 1)],
                                                   psb[kk % 2][:, :], AF.Identity), 1)

                for cc2 in range(2):
                    j2i = (n * 2 + cc2) % 2
                    jj = b * 32 + n * 2 + cc2

                    @block.vector
                    def _(v, b=b, n=n, cc2=cc2, j2i=j2i, jj=jj):
                        s_ae.wait(v, jj + 1)
                        s_bb.wait(v, jj + 1)
                        if jj >= 2:
                            s_yt.wait(v, jj - 1)
                        s_sn.inc(v.tensor_tensor_scan(h16[j2i][:, :], tmp16[j2i][:, :],
                                                      bb16[j2i][:, :], 0.0, ALU.mult, ALU.add), 1)
                        if jj % 32 == 0:
                            tails = []
                        elif jj % 32 == 31:
                            tails = [jj - 1, jj]
                        else:
                            tails = [jj - 1]
                        for ja in tails:
                            s_yt.wait(v, ja + 1)
                            s_ya.inc(v.tensor_tensor(yb[:, ja % 2, :], yb[:, ja % 2, :],
                                                     yt16[ja % 2][:, :], ALU.add), 1)

                    @block.gpsimd
                    def _(g, b=b, n=n, cc2=cc2, jj=jj, j2i=j2i, m=m):
                        if jj % 32 == 0:
                            pre = [jj, jj + 1]
                        elif jj % 32 == 31:
                            pre = []
                        else:
                            pre = [jj + 1]
                        for k in pre:
                            kcc2, km = k % 2, k // 2
                            s_be.wait(g, 16 * km + 8)
                            s_dx.wait(g, 4 * (k // 32) + 2 * (kcc2 + 1))
                            if k >= 2:
                                s_sn.wait(g, k - 1)
                            s_bb.inc(g.tensor_tensor(bb16[k % 2][:, :], xs16[:, kcc2, :],
                                                     Bmb[km % 2][:, :], ALU.mult), 1)
                        s_be.wait(g, 16 * m + 16)
                        s_sn.wait(g, jj + 1)
                        if jj >= 2:
                            s_ya.wait(g, jj - 1)
                        s_yt.inc(g.tensor_tensor(yt16[j2i][:, :], h16[j2i][:, :],
                                                 Cmb[m % 2][:, :], ALU.mult), 1)

            @block.gpsimd
            def _(g, b=b):
                s_ya.wait(g, 32 * (b + 1))
                for cc2 in range(2):
                    for hf in range(2):
                        s_yw.dinc(g.dma_start(
                            out=io["yc"][b + 4 * hf, 128 * cc2:128 * (cc2 + 1), :],
                            in_=yb[:, cc2, T * hf:T * (hf + 1)]))

        @block.gpsimd
        def _(g):
            s_yw.wait_abs(g)
        ctx.barrier()

    s_ag2 = ctx.gsem("smag2")

    @block.gpsimd
    def _(g):
        s_ag2.inc(g.collective_compute(
            "AllGather", ALU.bypass, replica_groups=[list(range(NCORE))],
            ins=[io["yc"].ap().opt()], outs=[io["ygf"].ap().opt()]), 1)
        s_ag2.wait(g, 1)
    ctx.barrier()

    with contextlib.ExitStack() as lst:
        ych = [ctx.sb("sm_ych", [128, T], BF16, lst) for _ in range(2)]
        stg = [ctx.sb("sm_stg", [128, T], F32, lst) for _ in range(2)]
        idxu2 = ctx.sb("sm_idxu2", [128, 16], U32, lst)
        s_i2 = ctx.sem("smi2")
        s_g2 = ctx.sem("smg2")
        s_cv = ctx.sem("smcv")
        s_pa = ctx.sem("smpa")

        @block.vector
        def _(v):
            for k in range(16):
                s_, cc = k // 2, k % 2
                s_i2.inc(v.tensor_scalar(idxu2[:, k:k + 1], IDXB[:, :],
                                         2048 * s_ + 128 * cc, None, ALU.add), 1)

        @block.gpsimd
        def _(g):
            for k in range(16):
                s_i2.wait(g, k + 1)
                if k >= 2:
                    s_cv.wait(g, k - 1)
                s_g2.dinc(g.indirect_dma_start(
                    out=ych[k % 2][:, :], out_offset=None, in_=io["ygf"].ap(),
                    in_offset=IndirectOffsetOnAxis(ap=idxu2[:, k:k + 1], axis=0)))
                if k >= 1:
                    s_cv.wait(g, k)
                    s_pa.dinc(g.dma_start(out=xs_out[k - 1, :, :],
                                          in_=stg[(k - 1) % 2][:, :], accum_op=ALU.add))
            s_cv.wait(g, 16)
            s_pa.dinc(g.dma_start(out=xs_out[15, :, :], in_=stg[15 % 2][:, :],
                                  accum_op=ALU.add))
            s_pa.wait_abs(g)

        @block.scalar
        def _(sc):
            for k in range(16):
                s_g2.wait(sc, 16 * (k + 1))
                if k >= 2:
                    s_pa.wait(sc, 16 * (k - 1))
                s_cv.inc(sc.activation(stg[k % 2][:, :], ych[k % 2][:, :], AF.Identity), 1)
    ctx.barrier()



def declare_common(nc, io):
    io["consts_t0"] = nc.dram_tensor("consts", [128, 8], F32, kind="ExternalInput").ap()
    io["ones_col0"] = nc.dram_tensor("ones_col", [128, 1], BF16, kind="ExternalInput").ap()
    io["ones_row0"] = nc.dram_tensor("ones_row", [1, 128], BF16, kind="ExternalInput").ap()
    io["ident0"] = nc.dram_tensor("ident", [128, 128], BF16, kind="ExternalInput").ap()


def emit_load_common(ctx, io):
    block = ctx.block
    io["consts_t"] = ctx.sb("consts_t", [128, 8], F32)
    io["ones_col_t"] = ctx.sb("ones_colt", [128, 1], BF16)
    io["ones_row_t"] = ctx.sb("ones_rowt", [1, 128], BF16)
    io["ident_t"] = ctx.sb("identt", [128, 128], BF16)
    s_cm = ctx.sem("cmn")

    @block.sync
    def _(sync):
        s_cm.dinc(sync.dma_start(out=io["consts_t"][:, :], in_=io["consts_t0"][:, :]))
        s_cm.dinc(sync.dma_start(out=io["ones_col_t"][:, :], in_=io["ones_col0"][:, :]))
        s_cm.dinc(sync.dma_start(out=io["ones_row_t"][:, :], in_=io["ones_row0"][:, :]))
        s_cm.dinc(sync.dma_start(out=io["ident_t"][:, :], in_=io["ident0"][:, :]))
        s_cm.wait(sync, 64)
    ctx.barrier()


def build_ffn0_pass(with_dequant=True):
    nc = bass.Bass("TRN2", target_bir_lowering=False)
    io = {}
    xs_in = nc.dram_tensor("xs_in", [NCH, 128, T], F32, kind="ExternalInput").ap()
    xs_out = nc.dram_tensor("xs_out", [NCH, 128, T], F32, kind="ExternalOutput").ap()
    for nm in ("g8", "u8", "d8"):
        io[nm] = nc.dram_tensor(nm, [1024, 2048], U8, kind="ExternalInput").ap()
    for nm in ("gsc", "usc", "dsc"):
        io[nm] = nc.dram_tensor(nm, [1024, 2048], F32, kind="ExternalInput").ap()
    declare_common(nc, io)
    io["ctg"] = nc.dram_tensor("ctg", [2, 128, NCH, 1024], BF16)
    io["ctu"] = nc.dram_tensor("ctu", [2, 128, NCH, 1024], BF16)
    io["ctd"] = nc.dram_tensor("ctd", [2, 128, 8, 2048], BF16)
    io["cg"] = nc.dram_tensor("cg", [16, 128, NCH, 1024], BF16)
    io["cu"] = nc.dram_tensor("cu", [16, 128, NCH, 1024], BF16)
    io["cd"] = nc.dram_tensor("cd", [16, 128, 8, 2048], BF16)

    with contextlib.ExitStack() as stack:
        ctx = Ctx(nc, stack)
        emit_load_common(ctx, io)
        emit_stream_copy(ctx, xs_in, xs_out)
        if with_dequant:
            emit_dequant(ctx, io)
        xn = ctx.sb("xn", [128, NCH, SLAB], BF16)
        for slab in range(2):
            emit_ln(ctx, io, xs_in, xn, 0, slab * SLAB, SLAB)
            emit_ffn0_slab(ctx, io, xn, xs_out, slab * SLAB)
    return nc


def common_in_map(host):
    return {
        "consts": host["consts"], "ones_col": host["ones_col"],
        "ones_row": host["ones_row"], "ident": host["ident"],
    }


def ffn0_in_map(host, r):
    m = common_in_map(host)
    m.update({
        "g8": host["g8"][r], "u8": host["u8"][r], "d8": host["d8"][r],
        "gsc": host["gsc"][r], "usc": host["usc"][r], "dsc": host["dsc"][r],
    })
    return m


def run_ffn0_pass(host, stream):
    nc = build_ffn0_pass(with_dequant=True)
    in_maps = []
    for r in range(NCORE):
        m = ffn0_in_map(host, r)
        m["xs_in"] = stream[r]
        in_maps.append(m)
    res = run_bass_kernel_spmd(nc, in_maps, core_ids=list(range(NCORE)))
    return [np.asarray(res.results[r]["xs_out"]) for r in range(NCORE)]


def build_attn_pass():
    nc = bass.Bass("TRN2", target_bir_lowering=False)
    io = {}
    xs_in = nc.dram_tensor("xs_in", [NCH, 128, T], F32, kind="ExternalInput").ap()
    xs_out = nc.dram_tensor("xs_out", [NCH, 128, T], F32, kind="ExternalOutput").ap()
    io["wq"] = nc.dram_tensor("wq", [D, 256], F32, kind="ExternalInput").ap()
    io["wkv"] = nc.dram_tensor("wkv", [D, 128], F32, kind="ExternalInput").ap()
    io["wo"] = nc.dram_tensor("wo", [256, D], F32, kind="ExternalInput").ap()
    io["bo"] = nc.dram_tensor("bo", [128, NCH], F32, kind="ExternalInput").ap()
    io["rc"] = nc.dram_tensor("rc", [128, S], F32, kind="ExternalInput").ap()
    io["rs"] = nc.dram_tensor("rs", [128, S], F32, kind="ExternalInput").ap()
    io["slopes"] = nc.dram_tensor("slopes", [128, 4], F32, kind="ExternalInput").ap()
    io["pswapf"] = nc.dram_tensor("pswapf", [128, 128], F32, kind="ExternalInput").ap()
    declare_common(nc, io)
    io["xnc"] = nc.dram_tensor("xnc", [NCH, 128, T], BF16)
    io["xg"] = nc.dram_tensor("xg", [NCORE, NCH, 128, T], BF16)
    io["Qd"] = nc.dram_tensor("Qd", [4, 256, S], BF16)
    io["opart"] = nc.dram_tensor("opart", [NCORE, NCH, 128, T], BF16)
    io["rsout"] = nc.dram_tensor("rsout", [NCH, 128, T], BF16)

    with contextlib.ExitStack() as stack:
        ctx = Ctx(nc, stack)
        emit_load_common(ctx, io)
        emit_stream_copy(ctx, xs_in, xs_out)
        with contextlib.ExitStack() as xst:
            xn = ctx.sb("xn", [128, NCH, T], BF16, xst)
            for slab in range(2):
                emit_ln(ctx, io, xs_in, xn, slab * SLAB, slab * SLAB, SLAB)
            emit_xn_gather(ctx, io, xn, io["xnc"], io["xg"])
        emit_attention(ctx, io, xs_out)
    return nc


def run_attn_pass(host, stream):
    nc = build_attn_pass()
    in_maps = []
    for r in range(NCORE):
        m = common_in_map(host)
        m.update({"wq": host["wq"][r], "wkv": host["wkv"][r], "wo": host["wo"][r],
                  "bo": host["bo"], "rc": host["rc"], "rs": host["rs"],
                  "slopes": host["slopes"][r], "pswapf": host["pswapf"],
                  "xs_in": stream[r]})
        in_maps.append(m)
    res = run_bass_kernel_spmd(nc, in_maps, core_ids=list(range(NCORE)))
    return [np.asarray(res.results[r]["xs_out"]) for r in range(NCORE)]


def build_ssm_pass():
    nc = bass.Bass("TRN2", target_bir_lowering=False)
    io = {}
    xs_in = nc.dram_tensor("xs_in", [NCH, 128, T], F32, kind="ExternalInput").ap()
    xs_out = nc.dram_tensor("xs_out", [NCH, 128, T], F32, kind="ExternalOutput").ap()
    io["winT"] = nc.dram_tensor("winT", [D, 96], F32, kind="ExternalInput").ap()
    io["wdtT"] = nc.dram_tensor("wdtT", [64, 256], F32, kind="ExternalInput").ap()
    io["bdt"] = nc.dram_tensor("bdt", [128, 2], F32, kind="ExternalInput").ap()
    io["alogT"] = nc.dram_tensor("alogT", [128, 32], F32, kind="ExternalInput").ap()
    io["Dp"] = nc.dram_tensor("Dp", [128, 2], F32, kind="ExternalInput").ap()
    io["idxbase"] = nc.dram_tensor("idxbase", [128, 1], I32, kind="ExternalInput").ap()
    declare_common(nc, io)
    io["onehot32"] = nc.dram_tensor("onehot32", [32, 32 * 128], BF16, kind="ExternalInput").ap()
    io["xnc2"] = nc.dram_tensor("xnc2", [NCH, 128, T], BF16)
    io["xg2f"] = nc.dram_tensor("xg2f", [NCORE * D, T], BF16)
    io["dlr_dram"] = nc.dram_tensor("dlr_dram", [64, 8 * T], BF16)
    io["yc"] = nc.dram_tensor("yc", [NCORE, 256, T], BF16)
    io["ygf"] = nc.dram_tensor("ygf", [NCORE * 2048, T], BF16)

    with contextlib.ExitStack() as stack:
        ctx = Ctx(nc, stack)
        emit_load_common(ctx, io)
        emit_stream_copy(ctx, xs_in, xs_out)
        with contextlib.ExitStack() as xst:
            xn = ctx.sb("xn", [128, NCH, T], BF16, xst)
            for slab in range(2):
                emit_ln(ctx, io, xs_in, xn, slab * SLAB, slab * SLAB, SLAB)
            emit_xn_gather(ctx, io, xn, io["xnc2"], io["xg2f"])
        emit_ssm(ctx, io, xs_out)
    return nc


def run_ssm_pass(host, stream):
    nc = build_ssm_pass()
    in_maps = []
    for r in range(NCORE):
        m = common_in_map(host)
        m.update({"winT": host["winT"], "wdtT": host["wdtT"][r], "bdt": host["bdt"][r],
                  "alogT": host["alogT"][r], "Dp": host["Dp"][r],
                  "idxbase": host["idxbase"][r], "onehot32": host["onehot32"],
                  "xs_in": stream[r]})
        in_maps.append(m)
    res = run_bass_kernel_spmd(nc, in_maps, core_ids=list(range(NCORE)))
    return [np.asarray(res.results[r]["xs_out"]) for r in range(NCORE)]


def build_ffn1_pass(parts="WASTB"):
    nc = bass.Bass("TRN2", target_bir_lowering=False)
    io = {}
    xs_in = nc.dram_tensor("xs_in", [NCH, 128, T], F32, kind="ExternalInput").ap()
    xs_out = nc.dram_tensor("xs_out", [NCH, 128, T], F32, kind="ExternalOutput").ap()
    io["wg1"] = nc.dram_tensor("wg1", [D, N1], F32, kind="ExternalInput").ap()
    io["wu1"] = nc.dram_tensor("wu1", [D, N1], F32, kind="ExternalInput").ap()
    io["wd1"] = nc.dram_tensor("wd1", [N1, D], F32, kind="ExternalInput").ap()
    declare_common(nc, io)

    with contextlib.ExitStack() as stack:
        ctx = Ctx(nc, stack)
        emit_load_common(ctx, io)
        emit_stream_copy(ctx, xs_in, xs_out)
        xn = ctx.sb("xn", [128, NCH, SLAB], BF16)
        for slab in range(2):
            emit_ln(ctx, io, xs_in, xn, 0, slab * SLAB, SLAB)
            emit_ffn1_slab(ctx, io, xn, xs_out, slab * SLAB, parts=parts)
    return nc


def run_ffn1_pass(host, stream, parts="WASTB"):
    nc = build_ffn1_pass(parts)
    in_maps = []
    for r in range(NCORE):
        m = common_in_map(host)
        m.update({"wg1": host["wg1"], "wu1": host["wu1"], "wd1": host["wd1"],
                  "xs_in": stream[r]})
        in_maps.append(m)
    res = run_bass_kernel_spmd(nc, in_maps, core_ids=list(range(NCORE)))
    return [np.asarray(res.results[r]["xs_out"]) for r in range(NCORE)]



def build_fused(np_=6):
    nc = bass.Bass("TRN2", target_bir_lowering=False)
    io = {}
    xs_in = nc.dram_tensor("xs_in", [NCH, 128, T], F32, kind="ExternalInput").ap()
    xs_out = nc.dram_tensor("xs_out", [NCH, 128, T], F32, kind="ExternalOutput").ap()
    sA = nc.dram_tensor("sA", [NCH, 128, T], F32).ap()
    sB = nc.dram_tensor("sB", [NCH, 128, T], F32).ap()
    for nm in ("g8", "u8", "d8"):
        io[nm] = nc.dram_tensor(nm, [1024, 2048], U8, kind="ExternalInput").ap()
    for nm in ("gsc", "usc", "dsc"):
        io[nm] = nc.dram_tensor(nm, [1024, 2048], F32, kind="ExternalInput").ap()
    io["ctg"] = nc.dram_tensor("ctg", [2, 128, NCH, 1024], BF16)
    io["ctu"] = nc.dram_tensor("ctu", [2, 128, NCH, 1024], BF16)
    io["ctd"] = nc.dram_tensor("ctd", [2, 128, 8, 2048], BF16)
    io["cg"] = nc.dram_tensor("cg", [16, 128, NCH, 1024], BF16)
    io["cu"] = nc.dram_tensor("cu", [16, 128, NCH, 1024], BF16)
    io["cd"] = nc.dram_tensor("cd", [16, 128, 8, 2048], BF16)
    io["wq"] = nc.dram_tensor("wq", [D, 256], F32, kind="ExternalInput").ap()
    io["wkv"] = nc.dram_tensor("wkv", [D, 128], F32, kind="ExternalInput").ap()
    io["wo"] = nc.dram_tensor("wo", [256, D], F32, kind="ExternalInput").ap()
    io["bo"] = nc.dram_tensor("bo", [128, NCH], F32, kind="ExternalInput").ap()
    io["rc"] = nc.dram_tensor("rc", [128, S], F32, kind="ExternalInput").ap()
    io["rs"] = nc.dram_tensor("rs", [128, S], F32, kind="ExternalInput").ap()
    io["slopes"] = nc.dram_tensor("slopes", [128, 4], F32, kind="ExternalInput").ap()
    io["pswapf"] = nc.dram_tensor("pswapf", [128, 128], F32, kind="ExternalInput").ap()
    io["xnc"] = nc.dram_tensor("xnc", [NCH, 128, T], BF16)
    io["xg"] = nc.dram_tensor("xg", [NCORE, NCH, 128, T], BF16)
    io["Qd"] = nc.dram_tensor("Qd", [4, 256, S], BF16)
    io["opart"] = nc.dram_tensor("opart", [NCORE, NCH, 128, T], BF16)
    io["rsout"] = nc.dram_tensor("rsout", [NCH, 128, T], BF16)
    io["wg1"] = nc.dram_tensor("wg1", [D, N1], F32, kind="ExternalInput").ap()
    io["wu1"] = nc.dram_tensor("wu1", [D, N1], F32, kind="ExternalInput").ap()
    io["wd1"] = nc.dram_tensor("wd1", [N1, D], F32, kind="ExternalInput").ap()
    io["winT"] = nc.dram_tensor("winT", [D, 96], F32, kind="ExternalInput").ap()
    io["wdtT"] = nc.dram_tensor("wdtT", [64, 256], F32, kind="ExternalInput").ap()
    io["bdt"] = nc.dram_tensor("bdt", [128, 2], F32, kind="ExternalInput").ap()
    io["alogT"] = nc.dram_tensor("alogT", [128, 32], F32, kind="ExternalInput").ap()
    io["Dp"] = nc.dram_tensor("Dp", [128, 2], F32, kind="ExternalInput").ap()
    io["idxbase"] = nc.dram_tensor("idxbase", [128, 1], I32, kind="ExternalInput").ap()
    io["onehot32"] = nc.dram_tensor("onehot32", [32, 32 * 128], BF16, kind="ExternalInput").ap()
    io["xnc2"] = nc.dram_tensor("xnc2", [NCH, 128, T], BF16)
    io["xg2f"] = nc.dram_tensor("xg2f", [NCORE * D, T], BF16)
    io["dlr_dram"] = nc.dram_tensor("dlr_dram", [64, 8 * T], BF16)
    io["yc"] = nc.dram_tensor("yc", [NCORE, 256, T], BF16)
    io["ygf"] = nc.dram_tensor("ygf", [NCORE * 2048, T], BF16)
    declare_common(nc, io)

    chain = [xs_in, sA, sB, sA, sB, sA, xs_out]
    with contextlib.ExitStack() as stack:
        ctx = Ctx(nc, stack)
        emit_load_common(ctx, io)

        def scoped(fn):
            with contextlib.ExitStack() as ps_:
                ctx.default_stack = ps_
                fn()
                ctx.default_stack = None

        def pass_ffn0(cur, nxt, with_dq):
            emit_stream_copy(ctx, cur, nxt)
            if with_dq:
                emit_dequant(ctx, io)
            xn = ctx.sb("xn", [128, NCH, SLAB], BF16)
            for slab in range(2):
                emit_ln(ctx, io, cur, xn, 0, slab * SLAB, SLAB)
                emit_ffn0_slab(ctx, io, xn, nxt, slab * SLAB)

        def pass_attn(cur, nxt):
            emit_stream_copy(ctx, cur, nxt)
            with contextlib.ExitStack() as xst:
                xn = ctx.sb("xn", [128, NCH, T], BF16, xst)
                for slab in range(2):
                    emit_ln(ctx, io, cur, xn, slab * SLAB, slab * SLAB, SLAB)
                emit_xn_gather(ctx, io, xn, io["xnc"], io["xg"])
            emit_attention(ctx, io, nxt)

        def pass_ffn1(cur, nxt):
            emit_stream_copy(ctx, cur, nxt)
            xn = ctx.sb("xn", [128, NCH, SLAB], BF16)
            for slab in range(2):
                emit_ln(ctx, io, cur, xn, 0, slab * SLAB, SLAB)
                emit_ffn1_slab(ctx, io, xn, nxt, slab * SLAB, parts="WASTB")

        def pass_ssm(cur, nxt):
            emit_stream_copy(ctx, cur, nxt)
            with contextlib.ExitStack() as xst:
                xn = ctx.sb("xn", [128, NCH, T], BF16, xst)
                for slab in range(2):
                    emit_ln(ctx, io, cur, xn, slab * SLAB, slab * SLAB, SLAB)
                emit_xn_gather(ctx, io, xn, io["xnc2"], io["xg2f"])
            emit_ssm(ctx, io, nxt)

        plist = [
            lambda a, b: pass_ffn0(a, b, True),
            pass_attn,
            lambda a, b: pass_ffn0(a, b, False),
            pass_ffn1,
            pass_ssm,
            pass_ffn1,
        ]
        for i in range(np_):
            cur = chain[i]
            nxt = xs_out if i == np_ - 1 else chain[i + 1]
            scoped(lambda c=cur, x=nxt, f=plist[i]: f(c, x))
    return nc


def fused_in_map(host, r):
    m = common_in_map(host)
    m.update({
        "g8": host["g8"][r], "u8": host["u8"][r], "d8": host["d8"][r],
        "gsc": host["gsc"][r], "usc": host["usc"][r], "dsc": host["dsc"][r],
        "wq": host["wq"][r], "wkv": host["wkv"][r], "wo": host["wo"][r],
        "bo": host["bo"], "rc": host["rc"], "rs": host["rs"],
        "slopes": host["slopes"][r], "pswapf": host["pswapf"],
        "wg1": host["wg1"], "wu1": host["wu1"], "wd1": host["wd1"],
        "winT": host["winT"], "wdtT": host["wdtT"][r], "bdt": host["bdt"][r],
        "alogT": host["alogT"][r], "Dp": host["Dp"][r],
        "idxbase": host["idxbase"][r], "onehot32": host["onehot32"],
        "xs_in": host["x"][r],
    })
    return m


def kernel(**inputs):
    host = prep_host(inputs)
    nc = build_fused()
    in_maps = [fused_in_map(host, r) for r in range(NCORE)]
    res = run_bass_kernel_spmd(nc, in_maps, core_ids=list(range(NCORE)))
    stream = [np.asarray(res.results[r]["xs_out"]) for r in range(NCORE)]
    return unprep_host(stream)


def kernel_unfused(**inputs):
    host = prep_host(inputs)
    stream = host["x"]
    stream = run_ffn0_pass(host, stream)
    stream = run_attn_pass(host, stream)
    stream = run_ffn0_pass(host, stream)
    stream = run_ffn1_pass(host, stream)
    stream = run_ssm_pass(host, stream)
    stream = run_ffn1_pass(host, stream)
    return unprep_host(stream)


# version 14 (final)
# speedup vs baseline: 1.0061x; 1.0061x over previous
import contextlib

import numpy as np
import ml_dtypes

import concourse.bass as bass
import concourse.mybir as mybir
from concourse.bass import IndirectOffsetOnAxis
from concourse.bass_utils import run_bass_kernel_spmd

F32 = mybir.dt.float32
BF16 = mybir.dt.bfloat16
I32 = mybir.dt.int32
U32 = mybir.dt.uint32
U8 = mybir.dt.uint8
AF = mybir.ActivationFunctionType
ALU = mybir.AluOpType
AX = mybir.AxisListType
bf16 = ml_dtypes.bfloat16

NCORE = 8
B, S, D = 4, 4096, 2048
T = 2048
NCH = 16
SLAB = 1024
N0 = 16384
N1 = 8192
NH, NKV, DH = 32, 8, 64

GELU_A = 1.5957691216057308
GELU_B = 0.07135481283579408

ENGINES = ("sync", "tensor", "vector", "scalar", "gpsimd")


def _perm(n):
    j = np.arange(n)
    c2, r = j >> 8, j & 255
    par, p = r >> 7, r & 127
    return (256 * c2 + 2 * p + par).astype(np.int64)


PERM = _perm(D)
PERM16K = _perm(N0)
IPERM = np.argsort(PERM)


def _c(a):
    return np.ascontiguousarray(a)



def prep_host(inputs):
    h = {}
    x = np.asarray(inputs["L0_attn_hidden_states"])
    xs = []
    for r in range(NCORE):
        b, hf = r % 4, r // 4
        sl = x[b, 2048 * hf:2048 * (hf + 1), :][:, PERM]
        xs.append(_c(sl.T.reshape(NCH, 128, T)).astype(np.float32, copy=False))
    h["x"] = xs

    def pack_u8(w_packed):
        return np.asarray(w_packed).view(np.uint8)[:, ::4]

    g8 = _c(pack_u8(inputs["L0_ffn_gate_W_packed"])[PERM16K].T)
    u8_ = _c(pack_u8(inputs["L0_ffn_up_W_packed"])[PERM16K].T)
    d8 = _c(pack_u8(inputs["L0_ffn_down_W_packed"])[PERM].T)
    gsc = np.repeat(np.asarray(inputs["L0_ffn_gate_scales"])[PERM16K].T, 16, axis=0)
    usc = np.repeat(np.asarray(inputs["L0_ffn_up_scales"])[PERM16K].T, 16, axis=0)
    dsc = np.repeat(np.asarray(inputs["L0_ffn_down_scales"])[PERM].T, 16, axis=0)
    h["g8"] = [_c(g8[:, 2048 * r:2048 * (r + 1)]) for r in range(NCORE)]
    h["u8"] = [_c(u8_[:, 2048 * r:2048 * (r + 1)]) for r in range(NCORE)]
    h["d8"] = [_c(d8[1024 * r:1024 * (r + 1), :]) for r in range(NCORE)]
    h["gsc"] = [_c(gsc[:, 2048 * r:2048 * (r + 1)].astype(np.float32)) for r in range(NCORE)]
    h["usc"] = [_c(usc[:, 2048 * r:2048 * (r + 1)].astype(np.float32)) for r in range(NCORE)]
    h["dsc"] = [_c(dsc[1024 * r:1024 * (r + 1), :].astype(np.float32)) for r in range(NCORE)]

    wq = np.asarray(inputs["L0_attn_Wq"])[PERM]
    wk = np.asarray(inputs["L0_attn_Wk"])[PERM]
    wv = np.asarray(inputs["L0_attn_Wv"])[PERM]
    h["wq"] = [_c(wq[:, 256 * r:256 * (r + 1)]) for r in range(NCORE)]
    h["wkv"] = [_c(np.concatenate([wk[:, 64 * r:64 * (r + 1)],
                                   wv[:, 64 * r:64 * (r + 1)]], axis=1)) for r in range(NCORE)]
    wo_p = np.asarray(inputs["L0_attn_Wo"])[:, PERM]
    h["wo"] = [_c(wo_p[256 * r:256 * (r + 1), :]) for r in range(NCORE)]
    h["bo"] = _c(np.asarray(inputs["L0_attn_bo"])[PERM].reshape(NCH, 128).T.astype(np.float32))

    inv = 10000.0 ** (-2.0 * np.arange(32) / 64.0)
    fr = np.arange(S, dtype=np.float64)[:, None] * inv[None, :]
    cos, sin = np.cos(fr).T, np.sin(fr).T
    ctab = np.empty((64, S), np.float64)
    stab = np.empty((64, S), np.float64)
    ctab[0::2] = cos; ctab[1::2] = cos
    stab[0::2] = -sin; stab[1::2] = sin
    s8 = 1.0 / np.sqrt(8.0)
    h["rc"] = _c(np.vstack([ctab, ctab]).astype(np.float32) * np.float32(s8))
    h["rs"] = _c(np.vstack([stab, stab]).astype(np.float32) * np.float32(s8))
    slopes = 2.0 ** (-(np.arange(1, NH + 1) * 8.0 / NH))
    h["slopes"] = [
        _c(np.tile((-slopes[4 * r:4 * r + 4] / 8.0).astype(np.float32), (128, 1)))
        for r in range(NCORE)
    ]
    psw = np.zeros((128, 128), np.float32)
    for i in range(128):
        psw[i ^ 1, i] = 1.0
    h["pswapf"] = psw

    h["wg1"] = _c(np.asarray(inputs["L1_ffn_w_gate"])[PERM])
    h["wu1"] = _c(np.asarray(inputs["L1_ffn_w_up"])[PERM])
    h["wd1"] = _c(np.asarray(inputs["L1_ffn_w_down"])[:, PERM])

    h["winT"] = _c(np.asarray(inputs["L1_attn_W_in"])[:, PERM].T)
    wdtT = np.asarray(inputs["L1_attn_W_dt"])[PERM].T
    h["wdtT"] = [_c(wdtT[:, 256 * r:256 * (r + 1)]) for r in range(NCORE)]
    bdt = np.asarray(inputs["L1_attn_b_dt"])[PERM]
    h["bdt"] = [_c(bdt[256 * r:256 * (r + 1)].reshape(2, 128).T.astype(np.float32)) for r in range(NCORE)]
    al = np.asarray(inputs["L1_attn_A_log"])[PERM]
    h["alogT"] = [_c(al[256 * r:256 * (r + 1)].reshape(2, 128, 16).transpose(1, 0, 2)
                     .reshape(128, 32).astype(np.float32)) for r in range(NCORE)]
    dd = np.asarray(inputs["L1_attn_D"])[PERM]
    h["Dp"] = [_c(dd[256 * r:256 * (r + 1)].reshape(2, 128).T.astype(np.float32)) for r in range(NCORE)]

    h["ones_col"] = np.ones((128, 1), bf16)
    h["ones_row"] = np.ones((1, 128), bf16)
    h["ident"] = np.eye(128, dtype=np.float32).astype(bf16)
    cst = np.zeros((128, 8), np.float32)
    cst[:, 0] = 1e-5
    cst[:, 1] = -4.0
    cst[:, 2] = -12.0
    h["consts"] = cst
    h["idxbase"] = [_c((256 * r + np.arange(128))[:, None].astype(np.int32)) for r in range(NCORE)]
    oh = np.zeros((32, 32, 128), np.float32)
    for j in range(32):
        oh[j, j, :] = 1.0
    h["onehot32"] = _c(oh.reshape(32, 32 * 128).astype(bf16))
    return h


def unprep_host(stream_shards):
    out = np.empty((B, S, D), np.float32)
    for r in range(NCORE):
        b, hf = r % 4, r // 4
        sl = np.asarray(stream_shards[r]).reshape(D, T).T
        out[b, 2048 * hf:2048 * (hf + 1), :] = sl[:, IPERM]
    return out



class Sem:
    __slots__ = ("h", "n", "base")

    def __init__(self, h, base=0):
        self.h = h
        self.n = base
        self.base = base

    def inc(self, instr, k=1):
        instr.then_inc(self.h, k)
        self.n += k
        return instr

    def dinc(self, instr):
        return self.inc(instr, 16)

    def wait(self, eng, v):
        eng.wait_ge(self.h, self.base + v)

    def wait_abs(self, eng):
        eng.wait_ge(self.h, self.n)


class Ctx:
    def __init__(self, nc, stack):
        self.nc = nc
        self.stack = stack
        self.block = stack.enter_context(nc.Block())
        self._names = {}
        self.s_bar = self.sem("bar")
        self.scr = self.sb("barscr", [1, 16], F32)

    def name(self, base):
        k = self._names.get(base, 0)
        self._names[base] = k + 1
        return f"{base}_{k}"

    def sem(self, base):
        pool = getattr(self, "_sempool", None)
        if pool is None:
            pool = self._sempool = {}
        prev = pool.get(base)
        if prev is None:
            s = Sem(self.stack.enter_context(self.nc.semaphore(self.name(base))))
        else:
            s = Sem(prev.h, base=prev.n)
        pool[base] = s
        return s

    def gsem(self, base):
        return Sem(self.stack.enter_context(self.nc.semaphore(self.name(base))))

    def sb(self, base, shape, dtype, stack=None):
        st = stack or getattr(self, "default_stack", None) or self.stack
        return st.enter_context(self.nc.sbuf_tensor(self.name(base), shape, dtype))

    def ps(self, base, shape, dtype=F32, stack=None):
        st = stack or getattr(self, "default_stack", None) or self.stack
        return st.enter_context(self.nc.psum_tensor(self.name(base), shape, dtype))

    def barrier(self):
        sb = self.s_bar
        target = sb.n + 5
        block = self.block
        scr = self.scr

        @block.sync
        def _(e):
            sb.inc(e.nop(), 1)
            e.wait_ge(sb.h, target)

        @block.tensor
        def _(e):
            sb.inc(e.nop(), 1)
            e.wait_ge(sb.h, target)

        @block.vector
        def _(e):
            sb.inc(e.memset(scr[:, 0:1], 0.0), 1)
            e.wait_ge(sb.h, target)

        @block.scalar
        def _(e):
            sb.inc(e.activation(scr[:, 1:2], scr[:, 1:2], AF.Identity), 1)
            e.wait_ge(sb.h, target)

        @block.gpsimd
        def _(e):
            sb.inc(e.memset(scr[:, 2:3], 0.0), 1)
            e.wait_ge(sb.h, target)

        assert sb.n == target



def emit_stream_copy(ctx, xs_in, xs_out):
    block = ctx.block
    with contextlib.ExitStack() as lst:
        bt = [ctx.sb("cp_b", [128, T], F32, lst) for _ in range(2)]
        s_l = ctx.sem("cpl")
        s_s = ctx.sem("cps")

        @block.sync
        def _(sync):
            for c in range(NCH):
                if c >= 2:
                    s_s.wait(sync, 16 * (c - 1))
                s_l.dinc(sync.dma_start(out=bt[c % 2][:, :], in_=xs_in[c, :, :]))

        @block.gpsimd
        def _(g):
            for c in range(NCH):
                s_l.wait(g, 16 * (c + 1))
                s_s.dinc(g.dma_start(out=xs_out[c, :, :], in_=bt[c % 2][:, :]))
            s_s.wait(g, 16 * NCH)
    ctx.barrier()



def emit_dequant(ctx, io):
    nc, block = ctx.nc, ctx.block
    with contextlib.ExitStack() as lst:
        vt = [ctx.sb("dq_v", [128, 2048], U8, lst) for _ in range(2)]
        sct = [ctx.sb("dq_sc", [128, 2048], F32, lst) for _ in range(2)]
        v32 = ctx.sb("dq_v32", [128, 2048], I32, lst)
        n32 = ctx.sb("dq_n32", [128, 2048], I32, lst)
        cf = ctx.sb("dq_cf", [128, 2048], F32, lst)
        bt_ = ctx.sb("dq_b", [128, 2048], F32, lst)
        ut = [ctx.sb("dq_u", [128, 2048], F32, lst) for _ in range(2)]
        zt = [ctx.sb("dq_z", [128, 2048], F32, lst) for _ in range(2)]
        r4t = [ctx.sb("dq_r4", [128, 2048], F32, lst) for _ in range(2)]
        r6t = [ctx.sb("dq_r6", [128, 2048], F32, lst) for _ in range(2)]
        tq = ctx.sb("dq_t", [128, 2048], F32, lst)
        mt = [ctx.sb("dq_m", [128, 2048], F32, lst) for _ in range(2)]
        wt = [ctx.sb("dq_w", [128, 2048], BF16, lst) for _ in range(2)]

        s_l = ctx.sem("dql")
        s_v = ctx.sem("dqv")
        s_s = ctx.sem("dqs")
        s_p = ctx.sem("dqp")
        s_w = ctx.sem("dqw")
        s_c = ctx.sem("dqc")
        s_ag = [ctx.gsem("dqag") for _ in range(3)]

        tiles = [(m, c2) for m in range(3) for c2 in range(8)]

        @block.sync
        def _(sync):
            for idx, (m, c2) in enumerate(tiles):
                if idx >= 2:
                    s_w.wait(sync, 2 * (idx - 1))
                src = (io["g8"], io["u8"], io["d8"])[m]
                ssc = (io["gsc"], io["usc"], io["dsc"])[m]
                s_l.dinc(sync.dma_start(out=vt[idx % 2][:, :], in_=src[128 * c2:128 * (c2 + 1), :]))
                s_l.dinc(sync.dma_start(out=sct[idx % 2][:, :], in_=ssc[128 * c2:128 * (c2 + 1), :]))

        def cum_contrib(idx):
            g_ = 4 * min(idx, 8) + 4 * min(max(idx - 8, 0), 8) + 2 * max(idx - 16, 0)
            return 16 * g_

        @block.vector
        def _(v):
            for idx, (m, c2) in enumerate(tiles):
                par = idx % 2
                s_l.wait(v, 32 * (idx + 1))
                if idx >= 1:
                    s_c.wait(v, cum_contrib(idx))
                v.tensor_copy(v32[:, :], vt[par][:, :])
                for nib in range(2):
                    j = 2 * idx + nib
                    if nib == 0:
                        v.tensor_scalar(n32[:, :], v32[:, :], 15, None, ALU.bitwise_and)
                    else:
                        v.tensor_scalar(n32[:, :], v32[:, :], 4, None, ALU.logical_shift_right)
                    v.tensor_copy(cf[:, :], n32[:, :])
                    v.tensor_scalar(bt_[:, :], cf[:, :], 8.0, None, ALU.is_ge)
                    v.scalar_tensor_tensor(ut[nib][:, :], bt_[:, :], -8.0, cf[:, :], ALU.mult, ALU.add)
                    s_v.inc(v.tensor_scalar(zt[nib][:, :], bt_[:, :], -1.0, 0.5, ALU.mult, ALU.add), 1)
                    s_p.wait(v, j + 1)
                    s_w.inc(v.tensor_tensor(wt[nib][:, :], mt[nib][:, :], sct[par][:, :], ALU.mult), 1)

        @block.scalar
        def _(sc):
            for idx in range(len(tiles)):
                for nib in range(2):
                    j = 2 * idx + nib
                    s_v.wait(sc, j + 1)
                    if j >= 2:
                        s_p.wait(sc, j - 1)
                    sc.activation(r4t[nib][:, :], ut[nib][:, :], AF.Relu, bias=io["consts_t"][:, 1:2])
                    s_s.inc(sc.activation(r6t[nib][:, :], ut[nib][:, :], AF.Relu,
                                          bias=io["consts_t"][:, 2:3], scale=2.0), 1)

        @block.gpsimd
        def _(g):
            for idx, (m, c2) in enumerate(tiles):
                for nib in range(2):
                    j = 2 * idx + nib
                    s_s.wait(g, j + 1)
                    g.tensor_tensor(tq[:, :], ut[nib][:, :], r4t[nib][:, :], ALU.add)
                    g.tensor_tensor(tq[:, :], tq[:, :], r6t[nib][:, :], ALU.add)
                    s_p.inc(g.tensor_tensor(mt[nib][:, :], tq[:, :], zt[nib][:, :], ALU.mult), 1)
                s_w.wait(g, 2 * (idx + 1))
                for nib in range(2):
                    kc = 2 * c2 + nib
                    if m < 2:
                        ct = io["ctg"] if m == 0 else io["ctu"]
                        for sl_ in range(2):
                            s_c.dinc(g.dma_start(
                                out=ct[sl_, :, kc, :],
                                in_=wt[nib][:, 1024 * sl_:1024 * (sl_ + 1)]))
                    else:
                        sl_, ic = kc // 8, kc % 8
                        s_c.dinc(g.dma_start(out=io["ctd"][sl_, :, ic, :], in_=wt[nib][:, :]))

            s_c.wait_abs(g)
            pairs = [(io["ctg"], io["cg"]), (io["ctu"], io["cu"]), (io["ctd"], io["cd"])]
            for i, (src, dst) in enumerate(pairs):
                s_ag[i].inc(g.collective_compute(
                    "AllGather", ALU.bypass,
                    replica_groups=[list(range(NCORE))],
                    ins=[src.ap().opt()], outs=[dst.ap().opt()]), 1)
            for i in range(3):
                s_ag[i].wait(g, 1)
    ctx.barrier()



def emit_ln(ctx, io, xs_in, xn, xoff, t0, tn):
    nc, block = ctx.nc, ctx.block
    ntt = tn // 512
    with contextlib.ExitStack() as lst:
        xf = ctx.sb("ln_xf", [128, NCH, tn], F32, lst)
        xb = [ctx.sb("ln_xb", [128, tn], BF16, lst) for _ in range(2)]
        xq = [ctx.sb("ln_xq", [128, tn], BF16, lst) for _ in range(2)]
        rows = ctx.sb("ln_rows", [1, 6 * tn], F32, lst)
        m16 = ctx.sb("ln_m16", [1, tn], BF16, lst)
        r16 = ctx.sb("ln_r16", [1, tn], BF16, lst)
        mb = ctx.sb("ln_mb", [128, tn], F32, lst)
        rb = ctx.sb("ln_rb", [128, tn], F32, lst)
        tmpf = ctx.sb("ln_tmp", [128, tn], F32, lst)
        pstat = [ctx.ps("ln_ps", [1, 512], F32, lst) for _ in range(2 * ntt)]
        pbc = [ctx.ps("ln_pbc", [128, 512], F32, lst) for _ in range(2)]

        s_x = ctx.sem("lnx")
        s_cast = ctx.sem("lncast")
        s_mm = ctx.sem("lnmm")
        s_row = ctx.sem("lnrow")
        s_act = ctx.sem("lnact")
        s_bcm = ctx.sem("lnbcm")
        s_bce = ctx.sem("lnbce")
        s_xn = ctx.sem("lnxn")

        SUM = [pstat[tt] for tt in range(ntt)]
        SQ = [pstat[ntt + tt] for tt in range(ntt)]
        MEAN = rows[:, 2 * tn:3 * tn]
        VAR = rows[:, 3 * tn:4 * tn]
        TMP = rows[:, 4 * tn:5 * tn]
        RSTD = rows[:, 5 * tn:6 * tn]

        @block.sync
        def _(sync):
            for c in range(NCH):
                s_x.dinc(sync.dma_start(out=xf[:, c, :], in_=xs_in[c, :, t0:t0 + tn]))

        @block.scalar
        def _(sc):
            for c in range(NCH):
                s_x.wait(sc, 16 * (c + 1))
                if c >= 2:
                    s_mm.wait(sc, c - 1)
                sc.activation(xb[c % 2][:, :], xf[:, c, :], AF.Identity)
                s_cast.inc(sc.activation(xq[c % 2][:, :], xf[:, c, :], AF.Square), 1)

        @block.tensor
        def _(te):
            for c in range(NCH):
                s_cast.wait(te, c + 1)
                last = None
                for tt in range(ntt):
                    te.matmul(SUM[tt][:, :], io["ones_col_t"][:, :],
                              xb[c % 2][:, 512 * tt:512 * (tt + 1)],
                              start=(c == 0), stop=(c == NCH - 1))
                    last = te.matmul(SQ[tt][:, :], io["ones_col_t"][:, :],
                                     xq[c % 2][:, 512 * tt:512 * (tt + 1)],
                                     start=(c == 0), stop=(c == NCH - 1))
                s_mm.inc(last, 1)

        @block.vector
        def _(v):
            s_mm.wait(v, NCH)
            for tt in range(ntt):
                sl_ = slice(512 * tt, 512 * (tt + 1))
                v.tensor_scalar(MEAN[:, sl_], SUM[tt][:, :], 1.0 / D, None, ALU.mult)
                v.tensor_scalar(TMP[:, sl_], SQ[tt][:, :], 1.0 / D, None, ALU.mult)
            v.tensor_tensor(VAR[:, :], MEAN[:, :], MEAN[:, :], ALU.mult)
            s_row.inc(v.tensor_tensor(VAR[:, :], TMP[:, :], VAR[:, :], ALU.subtract), 1)

        @block.scalar
        def _(sc):
            s_row.wait(sc, 1)
            sc.activation(TMP[:, :], VAR[:, :], AF.Ln, bias=io["consts_t"][0:1, 0:1])
            sc.activation(RSTD[:, :], TMP[:, :], AF.Exp, scale=-0.5)
            sc.activation(m16[:, :], MEAN[:, :], AF.Identity)
            s_act.inc(sc.activation(r16[:, :], RSTD[:, :], AF.Identity), 1)

        @block.tensor
        def _(te):
            s_act.wait(te, 1)
            jobs = []
            for tt in range(ntt):
                jobs.append((m16[:, 512 * tt:512 * (tt + 1)], tt))
            for tt in range(ntt):
                jobs.append((r16[:, 512 * tt:512 * (tt + 1)], ntt + tt))
            for j, (src, _) in enumerate(jobs):
                if j >= 2:
                    s_bce.wait(te, j - 1)
                s_bcm.inc(te.matmul(pbc[j % 2][:, :], io["ones_row_t"][:, :], src,
                                    start=True, stop=True), 1)

        @block.scalar
        def _(sc):
            for j in range(2 * ntt):
                s_bcm.wait(sc, j + 1)
                dst = mb if j < ntt else rb
                tt = j % ntt
                s_bce.inc(sc.activation(dst[:, 512 * tt:512 * (tt + 1)],
                                        pbc[j % 2][:, :], AF.Identity), 1)

        @block.vector
        def _(v):
            s_bce.wait(v, 2 * ntt)
            for c in range(NCH):
                v.tensor_tensor(tmpf[:, :], xf[:, c, :], mb[:, :], ALU.subtract)
                s_xn.inc(v.tensor_tensor(xn[:, c, xoff:xoff + tn], tmpf[:, :], rb[:, :],
                                         ALU.mult), 1)
    ctx.barrier()



def emit_ffn0_slab(ctx, io, xn, xs_out, t0):
    nc, block = ctx.nc, ctx.block
    NSUP = 16
    with contextlib.ExitStack() as lst:
        WG = ctx.sb("f0_wg", [128, NCH, 1024], BF16, lst)
        WU = ctx.sb("f0_wu", [128, NCH, 1024], BF16, lst)
        WD = ctx.sb("f0_wd", [128, 8, 2048], BF16, lst)
        INT = [ctx.sb("f0_int", [128, 8, 1024], BF16, lst) for _ in range(2)]
        gof = [ctx.sb("f0_gof", [128, 512], F32, lst) for _ in range(2)]
        us = [ctx.sb("f0_us", [128, 512], F32, lst) for _ in range(2)]
        sqt = [ctx.sb("f0_sq", [128, 512], F32, lst) for _ in range(2)]
        at = [ctx.sb("f0_at", [128, 512], F32, lst) for _ in range(2)]
        y2t = [ctx.sb("f0_y2", [128, 512], F32, lst) for _ in range(2)]
        sgt = [ctx.sb("f0_sg", [128, 512], F32, lst) for _ in range(2)]
        gut = [ctx.sb("f0_gu", [128, 512], F32, lst) for _ in range(2)]
        stg = [ctx.sb("f0_stg", [128, 512], F32, lst) for _ in range(4)]
        psG = [ctx.ps("f0_pg", [128, 512], F32, lst) for _ in range(3)]
        psU = [ctx.ps("f0_pu", [128, 512], F32, lst) for _ in range(3)]
        psD = [ctx.ps("f0_pd", [128, 512], F32, lst) for _ in range(2)]

        s_w = ctx.sem("f0w")
        s_phA = ctx.sem("f0A")
        s_phB = ctx.sem("f0B")
        s_ev = ctx.sem("f0ev")
        s_sq = ctx.sem("f0sq")
        s_y2 = ctx.sem("f0y2")
        s_sg = ctx.sem("f0sg")
        s_int = ctx.sem("f0int")
        s_fc = ctx.sem("f0fc")
        s_fd = ctx.sem("f0fd")

        TTn = 2

        @block.sync
        def _(sync):
            for sp in range(NSUP):
                if sp >= 1:
                    s_phA.wait(sync, 16 * sp)
                s_w.dinc(sync.dma_start(out=WG[:, :, :], in_=io["cg"][sp, :, :, :]))
                s_w.dinc(sync.dma_start(out=WU[:, :, :], in_=io["cu"][sp, :, :, :]))
                if sp >= 1:
                    s_phB.wait(sync, 32 * sp)
                s_w.dinc(sync.dma_start(out=WD[:, :, :], in_=io["cd"][sp, :, :, :]))

        @block.tensor
        def _(te):
            for sp in range(NSUP):
                s_w.wait(te, 48 * sp + 32)
                for idx in range(16):
                    ncc, tt = idx // TTn, idx % TTn
                    gidx = 16 * sp + idx
                    par3 = gidx % 3
                    if gidx >= 3:
                        s_ev.wait(te, gidx - 2)
                    for kc in range(NCH):
                        te.matmul(psG[par3][:, :], WG[:, kc, 128 * ncc:128 * (ncc + 1)],
                                  xn[:, kc, 512 * tt:512 * (tt + 1)],
                                  start=(kc == 0), stop=(kc == NCH - 1))
                    last = None
                    for kc in range(NCH):
                        last = te.matmul(psU[par3][:, :], WU[:, kc, 128 * ncc:128 * (ncc + 1)],
                                         xn[:, kc, 512 * tt:512 * (tt + 1)],
                                         start=(kc == 0), stop=(kc == NCH - 1))
                    s_phA.inc(last, 1)
                s_w.wait(te, 48 * (sp + 1))
                s_int.wait(te, 16 * (sp + 1))
                for idx in range(32):
                    hc, tt = idx // TTn, idx % TTn
                    par = idx % 2
                    gidx = 32 * sp + idx
                    if gidx >= 2:
                        s_fc.wait(te, gidx - 1)
                    last = None
                    for ic in range(8):
                        last = te.matmul(psD[par][:, :], WD[:, ic, 128 * hc:128 * (hc + 1)],
                                         INT[sp % 2][:, ic, 512 * tt:512 * (tt + 1)],
                                         start=(ic == 0), stop=(ic == 7))
                    s_phB.inc(last, 1)

        @block.scalar
        def _(sc):
            for sp in range(NSUP):
                for idx in range(16):
                    par = idx % 2
                    gidx = 16 * sp + idx
                    par3 = gidx % 3
                    s_phA.wait(sc, gidx + 1)
                    if gidx >= 2:
                        s_int.wait(sc, gidx - 1)
                    sc.activation(gof[par][:, :], psG[par3][:, :], AF.Identity)
                    s_ev.inc(sc.activation(us[par][:, :], psU[par3][:, :], AF.Identity), 1)
                    s_sq.inc(sc.activation(sqt[par][:, :], gof[par][:, :], AF.Square), 1)
                    s_y2.wait(sc, gidx + 1)
                    s_sg.inc(sc.activation(sgt[par][:, :], y2t[par][:, :], AF.Sigmoid), 1)
                for idx in range(32):
                    par = idx % 2
                    gidx = 32 * sp + idx
                    s_phB.wait(sc, gidx + 1)
                    if gidx >= 4:
                        s_fd.wait(sc, 16 * (gidx - 3))
                    s_fc.inc(sc.activation(stg[gidx % 4][:, :], psD[par][:, :],
                                           AF.Identity, scale=0.5), 1)

        @block.vector
        def _(v):
            for sp in range(NSUP):
                for idx in range(16):
                    par = idx % 2
                    gidx = 16 * sp + idx
                    s_sq.wait(v, gidx + 1)
                    v.tensor_scalar(at[par][:, :], sqt[par][:, :], GELU_B, GELU_A,
                                    ALU.mult, ALU.add)
                    s_y2.inc(v.tensor_tensor(y2t[par][:, :], at[par][:, :],
                                             gof[par][:, :], ALU.mult), 1)

        @block.gpsimd
        def _(g):
            for sp in range(NSUP):
                if sp >= 2:
                    s_phB.wait(g, 32 * (sp - 1))
                for idx in range(16):
                    ncc, tt = idx // TTn, idx % TTn
                    par = idx % 2
                    gidx = 16 * sp + idx
                    s_sg.wait(g, gidx + 1)
                    g.tensor_tensor(gut[par][:, :], gof[par][:, :], us[par][:, :], ALU.mult)
                    s_int.inc(g.tensor_tensor(INT[sp % 2][:, ncc, 512 * tt:512 * (tt + 1)],
                                              gut[par][:, :], sgt[par][:, :], ALU.mult), 1)
                for idx in range(32):
                    hc, tt = idx // TTn, idx % TTn
                    gidx = 32 * sp + idx
                    s_fc.wait(g, gidx + 1)
                    s_fd.dinc(g.dma_start(
                        out=xs_out[hc, :, t0 + 512 * tt:t0 + 512 * (tt + 1)],
                        in_=stg[gidx % 4][:, :], accum_op=ALU.add))
            s_fd.wait_abs(g)
    ctx.barrier()



def emit_ffn1_slab(ctx, io, xn, xs_out, t0, parts="ATB"):
    nc, block = ctx.nc, ctx.block
    NSUP = 8
    with contextlib.ExitStack() as lst:
        WG = ctx.sb("f1_wg", [128, NCH, 1024], BF16, lst)
        WU = ctx.sb("f1_wu", [128, NCH, 1024], BF16, lst)
        WD = ctx.sb("f1_wd", [128, 8, 2048], BF16, lst)
        HS = ctx.sb("f1_hs", [128, 8, 1024], BF16, lst)
        IT = ctx.sb("f1_it", [128, 8, SLAB], BF16, lst)
        sgt = [ctx.sb("f1_sg", [128, 512], F32, lst) for _ in range(2)]
        uft = [ctx.sb("f1_uf", [128, 512], F32, lst) for _ in range(2)]
        ht = [ctx.sb("f1_h", [128, 512], BF16, lst) for _ in range(2)]
        abt = [ctx.sb("f1_ab", [128, 512], BF16, lst) for _ in range(2)]
        mm1 = [ctx.sb("f1_m1", [128, 128], BF16, lst) for _ in range(2)]
        mm2 = [ctx.sb("f1_m2", [128, 128], BF16, lst) for _ in range(2)]
        t2t = [ctx.sb("f1_t2", [128, 128], BF16, lst) for _ in range(2)]
        get_ = [ctx.sb("f1_ge", [128, 512], BF16, lst) for _ in range(2)]
        stg = [ctx.sb("f1_stg", [128, 512], F32, lst) for _ in range(4)]
        psG = [ctx.ps("f1_pg", [128, 512], F32, lst) for _ in range(2)]
        psU = [ctx.ps("f1_pu", [128, 512], F32, lst) for _ in range(2)]
        psD = [ctx.ps("f1_pd", [128, 512], F32, lst) for _ in range(2)]
        psT = [ctx.ps("f1_pt", [128, 128], BF16, lst) for _ in range(2)]

        s_w = ctx.sem("f1w")
        s_phA = ctx.sem("f1A")
        s_phB = ctx.sem("f1B")
        s_ev = ctx.sem("f1ev")
        s_h = ctx.sem("f1h")
        s_tr = ctx.sem("f1tr")
        s_te = ctx.sem("f1te")
        s_fc = ctx.sem("f1fc")
        s_fd = ctx.sem("f1fd")

        @block.gpsimd
        def _(g):
            for sp in range(NSUP + 1):
                if sp < NSUP and ("A" in parts or "W" in parts):
                    if sp >= 1 and "A" in parts:
                        s_phA.wait(g, 16 * sp)
                    for kc in range(NCH):
                        s_w.dinc(g.dma_start(out=WG[:, kc, :],
                                             in_=io["wg1"][128 * kc:128 * (kc + 1),
                                                           1024 * sp:1024 * (sp + 1)]))
                    for kc in range(NCH):
                        s_w.dinc(g.dma_start(out=WU[:, kc, :],
                                             in_=io["wu1"][128 * kc:128 * (kc + 1),
                                                           1024 * sp:1024 * (sp + 1)]))
                if sp >= 1 and "B" in parts:
                    spf = sp - 1
                    for idx in range(32):
                        hc, tt2 = idx // 2, idx % 2
                        gidx = 32 * spf + idx
                        s_fc.wait(g, gidx + 1)
                        s_fd.dinc(g.dma_start(
                            out=xs_out[hc, :, t0 + 512 * tt2:t0 + 512 * (tt2 + 1)],
                            in_=stg[gidx % 4][:, :], accum_op=ALU.add))
                if sp < NSUP and "B" in parts:
                    if sp >= 1:
                        s_phB.wait(g, 32 * sp)
                    for ic in range(8):
                        s_w.dinc(g.dma_start(
                            out=WD[:, ic, :],
                            in_=io["wd1"][1024 * sp + 128 * ic:1024 * sp + 128 * (ic + 1), :]))
            s_fd.wait_abs(g)

        @block.tensor
        def _(te):
            for sp in range(NSUP):
                if "A" not in parts:
                    continue
                s_w.wait(te, 16 * ((40 if "B" in parts else 32) * sp + 32))
                for idx in range(16):
                    tc, it = idx // 2, idx % 2
                    par = idx % 2
                    gidx = 16 * sp + idx
                    if gidx >= 2:
                        s_ev.wait(te, gidx - 1)
                    for kc in range(NCH):
                        te.matmul(psG[par][:, :], xn[:, kc, 128 * tc:128 * (tc + 1)],
                                  WG[:, kc, 512 * it:512 * (it + 1)],
                                  start=(kc == 0), stop=(kc == NCH - 1))
                    last = None
                    for kc in range(NCH):
                        last = te.matmul(psU[par][:, :], xn[:, kc, 128 * tc:128 * (tc + 1)],
                                         WU[:, kc, 512 * it:512 * (it + 1)],
                                         start=(kc == 0), stop=(kc == NCH - 1))
                    s_phA.inc(last, 1)
                if "T" not in parts:
                    continue
                s_h.wait(te, 16 * (sp + 1))
                for j in range(64):
                    tc, ic = j // 8, j % 8
                    gj = 64 * sp + j
                    if gj >= 2:
                        s_te.wait(te, gj - 1)
                    s_tr.inc(te.transpose(psT[j % 2][:, :], HS[:, tc, 128 * ic:128 * (ic + 1)],
                                          io["ident_t"][:, :]), 1)
                if "B" not in parts:
                    continue
                s_w.wait(te, 16 * (40 * sp + 40))
                s_te.wait(te, 64 * (sp + 1))
                for idx in range(32):
                    hc, tt2 = idx // 2, idx % 2
                    par = idx % 2
                    gidx = 32 * sp + idx
                    if gidx >= 2:
                        s_fc.wait(te, gidx - 1)
                    last = None
                    for ic in range(8):
                        last = te.matmul(psD[par][:, :], WD[:, ic, 128 * hc:128 * (hc + 1)],
                                         IT[:, ic, 512 * tt2:512 * (tt2 + 1)],
                                         start=(ic == 0), stop=(ic == 7))
                    s_phB.inc(last, 1)

        @block.scalar
        def _(sc):
            for sp in range(NSUP):
                if "A" not in parts:
                    continue
                for idx in range(16):
                    par = idx % 2
                    gidx = 16 * sp + idx
                    s_phA.wait(sc, gidx + 1)
                    if gidx >= 2:
                        s_h.wait(sc, gidx - 1)
                    sc.activation(sgt[par][:, :], psG[par][:, :], AF.Silu)
                    s_ev.inc(sc.activation(uft[par][:, :], psU[par][:, :], AF.Identity), 1)
                if "T" in parts:
                    for j in range(64):
                        tc, ic = j // 8, j % 8
                        gj = 64 * sp + j
                        if j == 0 and sp >= 1 and "B" in parts:
                            s_phB.wait(sc, 32 * sp)
                        s_tr.wait(sc, gj + 1)
                        s_te.inc(sc.activation(IT[:, ic, 128 * tc:128 * (tc + 1)],
                                               psT[j % 2][:, :], AF.Identity), 1)
                if "B" in parts:
                    for idx in range(32):
                        par = idx % 2
                        gidx = 32 * sp + idx
                        s_phB.wait(sc, gidx + 1)
                        if gidx >= 4:
                            s_fd.wait(sc, 16 * (gidx - 3))
                        s_fc.inc(sc.activation(stg[gidx % 4][:, :], psD[par][:, :],
                                               AF.Identity, scale=0.5), 1)

        @block.vector
        def _(v):
            for sp in range(NSUP):
                if "A" not in parts:
                    continue
                for idx in range(16):
                    tc, it = idx // 2, idx % 2
                    par = idx % 2
                    gidx = 16 * sp + idx
                    s_ev.wait(v, gidx + 1)
                    if idx == 0 and sp >= 1 and "T" in parts:
                        s_tr.wait(v, 64 * sp)
                    v.tensor_tensor(ht[par][:, :], sgt[par][:, :], uft[par][:, :], ALU.mult)
                    if "S" not in parts:
                        s_h.inc(v.tensor_copy(HS[:, tc, 512 * it:512 * (it + 1)], ht[par][:, :]), 1)
                        continue
                    v.tensor_tensor(abt[par][:, :], ht[par][:, :], ht[par][:, :], ALU.mult)
                    A = abt[par][:, :].rearrange("p (g f) -> p g f", f=4)
                    M1 = mm1[par][:, :].rearrange("p (g o) -> p g o", o=1)
                    M2 = mm2[par][:, :].rearrange("p (g o) -> p g o", o=1)
                    v.tensor_tensor(M1, A[:, :, 0:1], A[:, :, 1:2], ALU.max)
                    v.tensor_tensor(M2, A[:, :, 2:3], A[:, :, 3:4], ALU.max)
                    v.tensor_tensor(t2t[par][:, :], mm1[par][:, :], mm2[par][:, :], ALU.min)
                    v.tensor_tensor(M1, A[:, :, 0:1], A[:, :, 1:2], ALU.min)
                    v.tensor_tensor(M2, A[:, :, 2:3], A[:, :, 3:4], ALU.min)
                    v.tensor_tensor(mm1[par][:, :], mm1[par][:, :], mm2[par][:, :], ALU.max)
                    v.tensor_tensor(t2t[par][:, :], t2t[par][:, :], mm1[par][:, :], ALU.max)
                    v.tensor_tensor(get_[par][:, :].rearrange("p (g f) -> p g f", f=4),
                                    A,
                                    t2t[par][:, :].rearrange("p (g o) -> p g o", o=1)
                                    .broadcast_to((128, 128, 4)),
                                    ALU.is_ge)
                    s_h.inc(v.tensor_tensor(HS[:, tc, 512 * it:512 * (it + 1)],
                                            ht[par][:, :], get_[par][:, :], ALU.mult), 1)
    ctx.barrier()



def emit_xn_gather(ctx, io, xn, contrib, gathered):
    block = ctx.block
    s_ct = ctx.sem("xgct")
    s_ag = ctx.gsem("xgag")

    @block.gpsimd
    def _(g):
        for c in range(NCH):
            s_ct.dinc(g.dma_start(out=contrib[c, :, :], in_=xn[:, c, :]))
        s_ct.wait_abs(g)
        s_ag.inc(g.collective_compute(
            "AllGather", ALU.bypass, replica_groups=[list(range(NCORE))],
            ins=[contrib.ap().opt()], outs=[gathered.ap().opt()]), 1)
        s_ag.wait(g, 1)
    ctx.barrier()



def emit_attention(ctx, io, xs_out):
    nc, block = ctx.nc, ctx.block
    NB = 4
    Kd = ctx.sb("at_kd", [128, NB * S], BF16)
    Vg = ctx.sb("at_vg", [128, NB * 32 * 65], BF16)
    WQ = ctx.sb("at_wq", [128, NCH, 256], BF16)
    WKV = ctx.sb("at_wkv", [128, NCH, 128], BF16)
    WO = ctx.sb("at_wo", [128, 2, D], BF16)
    PSW = ctx.sb("at_pswt", [128, 128], F32)
    RC = ctx.sb("at_rc", [128, S], F32)
    RS_ = ctx.sb("at_rst", [128, S], F32)
    SLP = ctx.sb("at_slp", [128, 4], F32)
    BOT = ctx.sb("at_bo", [128, NCH], F32)

    s_ld = ctx.sem("atld")
    s_ct = ctx.sem("atct")
    s_ag = ctx.sem("atag")

    @block.sync
    def _(sync):
        s_ld.dinc(sync.dma_start(out=PSW[:, :], in_=io["pswapf"][:, :]))
        s_ld.dinc(sync.dma_start(out=RC[:, :], in_=io["rc"][:, :]))
        s_ld.dinc(sync.dma_start(out=RS_[:, :], in_=io["rs"][:, :]))
        s_ld.dinc(sync.dma_start(out=SLP[:, :], in_=io["slopes"][:, :]))
        s_ld.dinc(sync.dma_start(out=BOT[:, :], in_=io["bo"][:, :]))

    @block.gpsimd
    def _(g):
        for kc in range(NCH):
            s_ld.dinc(g.dma_start(out=WQ[:, kc, :], in_=io["wq"][128 * kc:128 * (kc + 1), :]))
            s_ld.dinc(g.dma_start(out=WKV[:, kc, :], in_=io["wkv"][128 * kc:128 * (kc + 1), :]))
        for cc in range(2):
            s_ld.dinc(g.dma_start(out=WO[:, cc, :], in_=io["wo"][128 * cc:128 * (cc + 1), :]))
        g.memset(Vg[:, :], 1.0)
    ctx.barrier()

    with contextlib.ExitStack() as lst:
        xgt = [ctx.sb("at_xgt", [128, NCH, 512], BF16, lst) for _ in range(2)]
        QE = [[ctx.sb("at_qe", [128, 512], F32, lst) for _ in range(2)] for _ in range(2)]
        KE = [ctx.sb("at_ke", [64, 512], F32, lst) for _ in range(2)]
        VE = [ctx.sb("at_ve", [64, 512], BF16, lst) for _ in range(2)]
        T1 = ctx.sb("at_t1", [128, 512], F32, lst)
        T2 = ctx.sb("at_t2", [128, 512], F32, lst)
        QR = [ctx.sb("at_qr", [128, 512], BF16, lst) for _ in range(2)]
        psQ = [ctx.ps("at_pq", [128, 512], F32, lst) for _ in range(2)]
        psKV = ctx.ps("at_pkv", [128, 512], F32, lst)
        psSw = [ctx.ps("at_psw", [128, 512], F32, lst) for _ in range(2)]
        psSwK = ctx.ps("at_pswk", [64, 512], F32, lst)
        psT = [ctx.ps("at_pt", [128, 64], BF16, lst) for _ in range(2)]

        s_xg = ctx.sem("atxg")
        s_mm = ctx.sem("atmm")
        s_qe = ctx.sem("atqe")
        s_sw = ctx.sem("atsw")
        s_rp = ctx.sem("atrp")
        s_qd = ctx.sem("atqd")
        s_vte = ctx.sem("atvte")
        s_kdup = ctx.sem("atkdup")

        NIT = 32
        for i in range(NIT):
            s8, tch = i // 4, i % 4
            b, hf = s8 % 4, s8 // 4
            par = i % 2
            bcol = b * S + hf * T + 512 * tch
            tloc = hf * T + 512 * tch
            ib = b * 32 + hf * 16 + tch * 4

            @block.sync
            def _(sync, i=i, par=par, s8=s8, tch=tch):
                if i >= 2:
                    s_mm.wait(sync, 3 * (i - 1))
                for kc in range(NCH):
                    s_xg.dinc(sync.dma_start(out=xgt[par][:, kc, :],
                                             in_=io["xg"][s8, kc, :, 512 * tch:512 * (tch + 1)]))

            @block.tensor
            def _(te, i=i, par=par):
                s_xg.wait(te, 256 * (i + 1))
                if i == 0:
                    s_ld.wait_abs(te)
                if i >= 1:
                    s_qe.wait(te, 4 * i)
                for qc in range(2):
                    last = None
                    for kc in range(NCH):
                        last = te.matmul(psQ[qc][:, :], WQ[:, kc, 128 * qc:128 * (qc + 1)],
                                         xgt[par][:, kc, :], start=(kc == 0), stop=(kc == NCH - 1))
                    s_mm.inc(last, 1)
                last = None
                for kc in range(NCH):
                    last = te.matmul(psKV[:, :], WKV[:, kc, :], xgt[par][:, kc, :],
                                     start=(kc == 0), stop=(kc == NCH - 1))
                s_mm.inc(last, 1)
                s_qe.wait(te, 4 * (i + 1))
                if i >= 1:
                    s_rp.wait(te, 3 * i)
                s_sw.inc(te.matmul(psSw[0][:, :], PSW[:, :], QE[par][0][:, :],
                                   start=True, stop=True), 1)
                s_sw.inc(te.matmul(psSw[1][:, :], PSW[:, :], QE[par][1][:, :],
                                   start=True, stop=True), 1)
                s_sw.inc(te.matmul(psSwK[:, :], PSW[0:64, 0:64], KE[par][:, :],
                                   start=True, stop=True), 1)
                for j in range(4):
                    if 4 * i + j >= 2:
                        s_vte.wait(te, 4 * i + j - 1)
                    s_sw.inc(te.transpose(psT[j % 2][:, :], VE[par][:, 128 * j:128 * (j + 1)],
                                          io["ident_t"][0:64, 0:64]), 1)

            @block.scalar
            def _(sc, i=i, par=par, ib=ib):
                s_mm.wait(sc, 3 * (i + 1))
                if i >= 2:
                    s_rp.wait(sc, 3 * (i - 1))
                    s_sw.wait(sc, 7 * (i - 1))
                sc.activation(QE[par][0][:, :], psQ[0][:, :], AF.Identity)
                sc.activation(QE[par][1][:, :], psQ[1][:, :], AF.Identity)
                sc.activation(KE[par][:, :], psKV[0:64, :], AF.Identity)
                s_qe.inc(sc.activation(VE[par][:, :], psKV[64:128, :], AF.Identity), 4)
                for j in range(4):
                    s_sw.wait(sc, 7 * i + 4 + j)
                    s_vte.inc(sc.activation(Vg[:, 65 * (ib + j):65 * (ib + j) + 64],
                                            psT[j % 2][:, :], AF.Identity), 1)

            @block.vector
            def _(v, i=i, par=par, bcol=bcol, tloc=tloc):
                s_sw.wait(v, 7 * i + 3)
                if i >= 1:
                    s_qd.wait(v, 32 * i)
                for qc in range(2):
                    v.tensor_tensor(T1[:, :], QE[par][qc][:, :], RC[:, tloc:tloc + 512], ALU.mult)
                    v.tensor_tensor(T2[:, :], psSw[qc][:, :], RS_[:, tloc:tloc + 512], ALU.mult)
                    s_rp.inc(v.tensor_tensor(QR[qc][:, :], T1[:, :], T2[:, :], ALU.add), 1)
                v.tensor_tensor(T1[0:64, :], KE[par][:, :], RC[0:64, tloc:tloc + 512], ALU.mult)
                v.tensor_tensor(T2[0:64, :], psSwK[:, :], RS_[0:64, tloc:tloc + 512], ALU.mult)
                s_rp.inc(v.tensor_tensor(Kd[0:64, bcol:bcol + 512], T1[0:64, :],
                                         T2[0:64, :], ALU.add), 1)

            @block.sync
            def _(sync, i=i, s8=s8, tch=tch):
                b, hf = s8 % 4, s8 // 4
                s_rp.wait(sync, 3 * i + 2)
                for qc in range(2):
                    s_qd.dinc(sync.dma_start(
                        out=io["Qd"][b, 128 * qc:128 * (qc + 1),
                                     hf * T + 512 * tch:hf * T + 512 * (tch + 1)],
                        in_=QR[qc][:, :]))

            @block.gpsimd
            def _(g, i=i, bcol=bcol):
                s_rp.wait(g, 3 * (i + 1))
                s_kdup.inc(g.tensor_copy(Kd[64:128, bcol:bcol + 512],
                                         Kd[0:64, bcol:bcol + 512]), 1)

        @block.sync
        def _(sync):
            s_qd.wait_abs(sync)

        @block.gpsimd
        def _(g):
            s_kdup.wait_abs(g)
        ctx.barrier()

    late_stack = contextlib.ExitStack()
    CTX = ctx.sb("at_ctxv", [128, 2, NB * S], BF16, late_stack)
    with contextlib.ExitStack() as lst:
        qt = [ctx.sb("at_qt", [128, 512], BF16, lst) for _ in range(4)]
        dif = [ctx.sb("at_dif", [128, 512], F32, lst) for _ in range(2)]
        alit = [ctx.sb("at_ali", [128, 512], F32, lst) for _ in range(2)]
        msk = [ctx.sb("at_msk", [128, 512], BF16, lst) for _ in range(2)]
        s1t = [ctx.sb("at_s1", [128, 512], F32, lst) for _ in range(4)]
        tt_ = [ctx.sb("at_tt", [128, 512], F32, lst) for _ in range(2)]
        prb = [ctx.sb("at_prb", [128, 512], BF16, lst) for _ in range(4)]
        ce = [ctx.sb("at_ce", [128, 512], F32, lst) for _ in range(2)]
        srow = ctx.sb("at_srow", [1, 512], F32, lst)
        r16 = ctx.sb("at_r16", [1, 512], BF16, lst)
        psS = [ctx.ps("at_pss", [128, 512], F32, lst) for _ in range(4)]
        psC = [ctx.ps("at_pc", [128, 512], F32, lst) for _ in range(4)]

        s_q = ctx.sem("atq")
        s_sc = ctx.sem("atsc")
        s_s1 = ctx.sem("ats1")
        s_ex = ctx.sem("atex")
        s_mk = ctx.sem("atmk")
        s_av = ctx.sem("atav")
        s_dg = ctx.sem("atdg")
        s_al = ctx.sem("atal")
        s_ce = ctx.sem("atce")
        s_rc2 = ctx.sem("atrc2")
        s_r16 = ctx.sem("atr16")
        s_bc = ctx.sem("atbc")
        s_cnw = ctx.sem("atcnw")

        ht_c = 0
        tg_c = 0
        dg_c = 0
        it_c = 0
        meta = []
        for b in range(NB):
            for qch in range(8):
                nkt = 4 * qch + 4
                q0 = 512 * qch
                base_ht = ht_c
                base_tg = tg_c
                base_it = it_c

                @block.sync
                def _(sync, b=b, qch=qch, base_ht=base_ht):
                    if base_ht > 0:
                        s_sc.wait(sync, base_ht)
                    for h in range(4):
                        for half in range(2):
                            s_q.dinc(sync.dma_start(
                                out=qt[h][64 * half:64 * (half + 1), :],
                                in_=io["Qd"][b, 64 * h:64 * (h + 1), 512 * qch:512 * (qch + 1)]))

                @block.gpsimd
                def _(g, b=b, qch=qch, nkt=nkt, q0=q0, base_tg=base_tg, base_ht=base_ht,
                      base_dg=dg_c):
                    mkc = base_dg
                    for ktj in range(nkt):
                        tg = base_tg + ktj
                        diag = 128 * (ktj + 1) > q0
                        if tg >= 2:
                            s_al.wait(g, tg - 1)
                        g.iota(dif[tg % 2][:, :], [[1, 512]], base=q0 - 128 * ktj,
                               channel_multiplier=-1, allow_small_or_imprecise_dtypes=True)
                        if diag:
                            g.tensor_scalar(msk[tg % 2][:, :], dif[tg % 2][:, :], 0.0,
                                            None, ALU.is_ge)
                        s_dg.inc(g.memset(ctx.scr[:, 4:5], 0.0), 1)
                        if diag:
                            for h in range(4):
                                ht = base_ht + 4 * ktj + h
                                s_ex.wait(g, ht + 1)
                                mkc += 1
                                s_mk.inc(g.tensor_tensor(prb[ht % 4][:, :], prb[ht % 4][:, :],
                                                         msk[tg % 2][:, :], ALU.mult), 1)

                @block.scalar
                def _(sc, nkt=nkt, q0=q0, base_tg=base_tg, base_ht=base_ht):
                    for ktj in range(nkt):
                        tg = base_tg + ktj
                        s_dg.wait(sc, tg + 1)
                        s_al.inc(sc.activation(alit[tg % 2][:, :], dif[tg % 2][:, :], AF.Abs), 1)
                        for h in range(4):
                            ht = base_ht + 4 * ktj + h
                            s_s1.wait(sc, ht + 1)
                            if ht >= 4:
                                s_av.wait(sc, ht - 3)
                            sc.activation(tt_[ht % 2][:, :], s1t[ht % 4][:, :], AF.Tanh,
                                          scale=1.0 / 30.0)
                            s_ex.inc(sc.activation(prb[ht % 4][:, :], tt_[ht % 2][:, :],
                                                   AF.Exp, scale=30.0), 1)

                @block.vector
                def _(v, nkt=nkt, base_tg=base_tg, base_ht=base_ht):
                    for ktj in range(nkt):
                        tg = base_tg + ktj
                        s_al.wait(v, tg + 1)
                        for h in range(4):
                            ht = base_ht + 4 * ktj + h
                            s_sc.wait(v, ht + 1)
                            if ht >= 4:
                                s_ex.wait(v, ht - 3)
                            s_s1.inc(v.scalar_tensor_tensor(
                                s1t[ht % 4][:, :], alit[tg % 2][:, :], SLP[:, h:h + 1],
                                psS[ht % 4][:, :], ALU.mult, ALU.add), 1)

                @block.tensor
                def _(te, b=b, qch=qch, nkt=nkt, base_tg=base_tg, base_ht=base_ht,
                      base_it=base_it, meta=meta):
                    if base_ht == 0:
                        s_ld.wait_abs(te)
                    def emit_av(ht2):
                        is_diag, mkcum, h2, ib2, first, last_ = meta[ht2]
                        if is_diag:
                            s_mk.wait(te, mkcum)
                        else:
                            s_ex.wait(te, ht2 + 1)
                        m2 = ht2 % 4
                        ins = te.matmul(psC[h2][0:65, :], Vg[:, 65 * ib2:65 * (ib2 + 1)],
                                        prb[m2][:, :], start=first, stop=last_)
                        s_av.inc(ins, 1)
                    for ktj in range(nkt):
                        tg = base_tg + ktj
                        for h in range(4):
                            ht = base_ht + 4 * ktj + h
                            diag = 128 * (ktj + 1) > 512 * qch
                            mkcum = (meta[-1][1] if meta else 0) + (1 if diag else 0)
                            meta.append((diag, mkcum, h, b * 32 + ktj, ktj == 0, ktj == nkt - 1))
                            s_q.wait(te, 128 * (base_it + 1))
                            if ht >= 4:
                                s_s1.wait(te, ht - 3)
                            if ktj == 0:
                                s_ce.wait(te, 2 * 4 * base_it)
                                s_cnw.wait(te, 4 * base_it)
                            s_sc.inc(te.matmul(psS[ht % 4][:, :],
                                               Kd[0:64, b * S + 128 * ktj:b * S + 128 * (ktj + 1)],
                                               qt[h][0:64, :], start=True, stop=True), 1)
                            if ht >= 4:
                                emit_av(ht - 4)
                    for ht2 in range(base_ht + 4 * nkt - 4, base_ht + 4 * nkt):
                        emit_av(ht2)
                    for h in range(4):
                        s_r16.wait(te, 4 * base_it + h + 1)
                        if base_it > 0 or h >= 1:
                            s_cnw.wait(te, 4 * base_it + h)
                        s_bc.inc(te.matmul(psS[0][0:64, :], io["ones_row_t"][:, 0:64],
                                           r16[:, :], start=True, stop=True), 1)

                @block.scalar
                def _(sc, base_ht=base_ht, nkt=nkt, base_it=base_it):
                    for h in range(4):
                        s_av.wait(sc, base_ht + 4 * nkt)
                        if base_it > 0 or h >= 2:
                            s_cnw.wait(sc, 4 * base_it + h - 1)
                        sc.activation(ce[h % 2][0:65, :], psC[h][0:65, :], AF.Identity)
                        s_ce.inc(sc.activation(srow[:, :], ce[h % 2][64:65, :], AF.Identity), 2)
                        s_rc2.wait(sc, 4 * base_it + h + 1)
                        if base_it > 0 or h >= 1:
                            s_bc.wait(sc, 4 * base_it + h)
                        s_r16.inc(sc.activation(r16[:, :], srow[:, :], AF.Identity), 1)

                @block.vector
                def _(v, b=b, qch=qch, base_it=base_it):
                    for h in range(4):
                        s_ce.wait(v, 2 * (4 * base_it + h + 1))
                        s_rc2.inc(v.reciprocal(srow[:, :], srow[:, :]), 1)
                        s_bc.wait(v, 4 * base_it + h + 1)
                        s_cnw.inc(v.tensor_tensor(
                            CTX[64 * (h % 2):64 * (h % 2) + 64, h // 2,
                                b * S + 512 * qch:b * S + 512 * (qch + 1)],
                            ce[h % 2][0:64, :], psS[0][0:64, :], ALU.mult), 1)

                ht_c += 4 * nkt
                tg_c += nkt
                dg_c += 16
                it_c += 1
        ctx.barrier()

        if "ctx_o" in io:
            s_do = ctx.sem("atdo")

            @block.gpsimd
            def _(g):
                for cc in range(2):
                    s_do.dinc(g.dma_start(out=io["ctx_o"][:, cc, :], in_=CTX[:, cc, :]))
                s_do.wait_abs(g)
            ctx.barrier()

    with contextlib.ExitStack() as lst:
        pso = [ctx.ps("at_po", [128, 512], F32, lst) for _ in range(2)]
        ost = [ctx.sb("at_ost", [128, 512], BF16, lst) for _ in range(4)]
        rch = [ctx.sb("at_rch", [128, T], BF16, lst) for _ in range(2)]
        stg2 = [ctx.sb("at_stg2", [128, T], F32, lst) for _ in range(2)]
        s_om = ctx.sem("atom")
        s_oe = ctx.sem("atoe")
        s_od = ctx.sem("atod")
        s_rs = ctx.gsem("atrsc")
        s_rl = ctx.sem("atrl")
        s_pa = ctx.sem("atpa")

        @block.tensor
        def _(te):
            k = 0
            for s8 in range(8):
                for dc in range(NCH):
                    for tt2 in range(4):
                        if k >= 2:
                            s_oe.wait(te, k - 1)
                        col0 = (s8 % 4) * S + (s8 // 4) * T + 512 * tt2
                        last = None
                        for cc in range(2):
                            last = te.matmul(pso[k % 2][:, :], WO[:, cc, 128 * dc:128 * (dc + 1)],
                                             CTX[:, cc, col0:col0 + 512],
                                             start=(cc == 0), stop=(cc == 1))
                        s_om.inc(last, 1)
                        k += 1

        @block.scalar
        def _(sc):
            for k in range(8 * NCH * 4):
                s_om.wait(sc, k + 1)
                if k >= 4:
                    s_od.wait(sc, 16 * (k - 3))
                s_oe.inc(sc.activation(ost[k % 4][:, :], pso[k % 2][:, :], AF.Identity), 1)

        @block.sync
        def _(sync):
            k = 0
            for s8 in range(8):
                for dc in range(NCH):
                    for tt2 in range(4):
                        s_oe.wait(sync, k + 1)
                        s_od.dinc(sync.dma_start(
                            out=io["opart"][s8, dc, :, 512 * tt2:512 * (tt2 + 1)],
                            in_=ost[k % 4][:, :]))
                        k += 1
            s_od.wait_abs(sync)

        @block.scalar
        def _(sc):
            base = 8 * NCH * 4
            for c in range(NCH):
                s_rl.wait(sc, 16 * (c + 1))
                if c >= 2:
                    s_pa.wait(sc, 16 * (c - 1))
                s_oe.inc(sc.activation(stg2[c % 2][:, :], rch[c % 2][:, :], AF.Identity,
                                       bias=BOT[:, c:c + 1]), 1)

        @block.gpsimd
        def _(g):
            base = 8 * NCH * 4
            s_od.wait(g, 16 * 8 * NCH * 4)
            s_rs.inc(g.collective_compute(
                "ReduceScatter", ALU.add, replica_groups=[list(range(NCORE))],
                ins=[io["opart"].ap().opt()], outs=[io["rsout"].ap().opt()]), 1)
            s_rs.wait(g, 1)
            for c in range(NCH):
                s_rl.dinc(g.dma_start(out=rch[c % 2][:, :], in_=io["rsout"][c, :, :]))
                if c >= 1:
                    s_oe.wait(g, base + c)
                    s_pa.dinc(g.dma_start(out=xs_out[c - 1, :, :], in_=stg2[(c - 1) % 2][:, :],
                                          accum_op=ALU.add))
            s_oe.wait(g, base + NCH)
            s_pa.dinc(g.dma_start(out=xs_out[NCH - 1, :, :], in_=stg2[(NCH - 1) % 2][:, :],
                                  accum_op=ALU.add))
            s_pa.wait_abs(g)
    ctx.barrier()
    late_stack.close()



def emit_ssm(ctx, io, xs_out):
    nc, block = ctx.nc, ctx.block
    NB = 4
    WIN = ctx.sb("sm_win", [128, NCH, 96], BF16)
    WDT = ctx.sb("sm_wdt", [64, 256], BF16)
    BDT = ctx.sb("sm_bdt", [128, 2], F32)
    AEXP = ctx.sb("sm_aexp", [128, 32], F32)
    DPT = ctx.sb("sm_dpt", [128, 2], F32)
    IDXB = ctx.sb("sm_idxb", [128, 1], I32)
    OH = ctx.sb("sm_oh", [32, 32 * 128], BF16)
    BC16 = ctx.sb("sm_bc16", [32, 8 * T], BF16)

    s_ld = ctx.sem("smld")

    @block.gpsimd
    def _(g):
        for kc in range(NCH):
            s_ld.dinc(g.dma_start(out=WIN[:, kc, :], in_=io["winT"][128 * kc:128 * (kc + 1), :]))
        s_ld.dinc(g.dma_start(out=WDT[:, :], in_=io["wdtT"][:, :]))

    @block.sync
    def _(sync):
        s_ld.dinc(sync.dma_start(out=BDT[:, :], in_=io["bdt"][:, :]))
        s_ld.dinc(sync.dma_start(out=AEXP[:, :], in_=io["alogT"][:, :]))
        s_ld.dinc(sync.dma_start(out=DPT[:, :], in_=io["Dp"][:, :]))
        s_ld.dinc(sync.dma_start(out=IDXB[:, :], in_=io["idxbase"][:, :]))
        s_ld.dinc(sync.dma_start(out=OH[:, :], in_=io["onehot32"][:, :]))

    @block.scalar
    def _(sc):
        s_ld.wait_abs(sc)
        sc.activation(AEXP[:, :], AEXP[:, :], AF.Exp)
        sc.activation(AEXP[:, :], AEXP[:, :], AF.Identity, scale=-1.0)
    ctx.barrier()

    with contextlib.ExitStack() as lst:
        xgt = ctx.sb("sm_xgt", [128, NCH, 512], BF16, lst)
        DLRS = ctx.sb("sm_dlrs", [64, 512], BF16, lst)
        psD_ = [ctx.ps("sm_psd", [128, 512], F32, lst) for _ in range(2)]
        s_xg = ctx.sem("smxg")
        s_mm = ctx.sem("smmm")
        s_ev = ctx.sem("smev")
        s_dw = ctx.sem("smdw")

        for i in range(32):
            s8, tt = i // 4, i % 4

            @block.sync
            def _(sync, i=i, s8=s8, tt=tt):
                if i >= 1:
                    s_mm.wait(sync, i)
                for kc in range(NCH):
                    r0 = 2048 * s8 + 128 * kc
                    s_xg.dinc(sync.dma_start(out=xgt[:, kc, :],
                                             in_=io["xg2f"][r0:r0 + 128, 512 * tt:512 * (tt + 1)]))

            @block.tensor
            def _(te, i=i):
                s_xg.wait(te, 256 * (i + 1))
                if i == 0:
                    s_ld.wait_abs(te)
                if i >= 2:
                    s_ev.wait(te, 2 * (i - 1))
                last = None
                for kc in range(NCH):
                    last = te.matmul(psD_[i % 2][0:96, :], WIN[:, kc, :], xgt[:, kc, :],
                                     start=(kc == 0), stop=(kc == NCH - 1))
                s_mm.inc(last, 1)

            @block.scalar
            def _(sc, i=i, s8=s8, tt=tt):
                s_mm.wait(sc, i + 1)
                if i >= 1:
                    s_dw.wait(sc, 16 * i)
                col = 2048 * s8 + 512 * tt
                sc.activation(DLRS[:, :], psD_[i % 2][0:64, :], AF.Identity)
                s_ev.inc(sc.activation(BC16[:, col:col + 512], psD_[i % 2][64:96, :],
                                       AF.Identity), 2)

            @block.gpsimd
            def _(g, i=i, s8=s8, tt=tt):
                s_ev.wait(g, 2 * (i + 1))
                col = 2048 * s8 + 512 * tt
                s_dw.dinc(g.dma_start(out=io["dlr_dram"][:, col:col + 512], in_=DLRS[:, :]))

        @block.gpsimd
        def _(g):
            s_dw.wait_abs(g)
        ctx.barrier()

    with contextlib.ExitStack() as lst:
        xs16 = ctx.sb("sm_xs", [128, 2, 2 * T], BF16, lst)
        d16 = ctx.sb("sm_d16", [128, 2, 2 * T], BF16, lst)
        DLRB = ctx.sb("sm_dlrb", [64, 2 * T], BF16, lst)
        tmp16 = [ctx.sb("sm_tmp", [128, 2 * T], BF16, lst) for _ in range(2)]
        bb16 = [ctx.sb("sm_bb", [128, 2 * T], BF16, lst) for _ in range(2)]
        h16 = [ctx.sb("sm_h", [128, 2 * T], BF16, lst) for _ in range(2)]
        yt16 = [ctx.sb("sm_yt", [128, 2 * T], BF16, lst) for _ in range(2)]
        yb = ctx.sb("sm_yb", [128, 2, 2 * T], BF16, lst)
        Bmb = [ctx.sb("sm_bmb", [128, 2 * T], BF16, lst) for _ in range(2)]
        Cmb = [ctx.sb("sm_cmb", [128, 2 * T], BF16, lst) for _ in range(2)]
        idxu = ctx.sb("sm_idxu", [128, 4], U32, lst)
        hcol = [ctx.sb("sm_hcol", [128, 1], F32, lst) for _ in range(2)]
        io_dbg_h_tile = None
        Xsp = ctx.sb("sm_xsp", [128, 512], F32, lst)
        Rsp = ctx.sb("sm_rsp", [128, 512], F32, lst)
        Asp = ctx.sb("sm_asp", [128, 512], F32, lst)
        psd2 = [ctx.ps("sm_psd2", [128, 512], F32, lst) for _ in range(2)]
        psb = [ctx.ps("sm_psb", [128, 512], F32, lst) for _ in range(2)]

        s_idx = ctx.sem("smidx")
        s_gx = ctx.sem("smgx")
        s_dl = ctx.sem("smdl")
        s_dm = ctx.sem("smdm")
        s_de = ctx.sem("smde")
        s_dc = ctx.sem("smdc")
        s_dx = ctx.sem("smdx")
        s_bm = ctx.sem("smbm")
        s_be = ctx.sem("smbe")
        s_tm = ctx.sem("smtm")
        s_ae = ctx.sem("smae")
        s_bb = ctx.sem("smbb")
        s_sn = ctx.sem("smsn")
        s_h1 = ctx.sem("smh1")
        s_h2 = ctx.sem("smh2")
        s_yt = ctx.sem("smyt")
        s_ya = ctx.sem("smya")
        s_yw = ctx.sem("smyw")

        for b in range(NB):
            @block.vector
            def _(v, b=b):
                if b >= 1:
                    s_gx.wait(v, 64 * b)
                for cc2 in range(2):
                    for hf in range(2):
                        s_idx.inc(v.tensor_scalar(idxu[:, 2 * cc2 + hf:2 * cc2 + hf + 1],
                                                  IDXB[:, :], 2048 * (b + 4 * hf) + 128 * cc2,
                                                  None, ALU.add), 1)

            @block.gpsimd
            def _(g, b=b):
                for cc2 in range(2):
                    for hf in range(2):
                        s_idx.wait(g, 4 * b + 2 * cc2 + hf + 1)
                        if b >= 1:
                            s_dx.wait(g, 4 * b)
                            s_bb.wait(g, 32 * b)
                        s_gx.dinc(g.indirect_dma_start(
                            out=xs16[:, cc2, T * hf:T * (hf + 1)], out_offset=None,
                            in_=io["xg2f"].ap(),
                            in_offset=IndirectOffsetOnAxis(
                                ap=idxu[:, 2 * cc2 + hf:2 * cc2 + hf + 1], axis=0)))

            @block.sync
            def _(sync, b=b):
                if b >= 1:
                    s_dm.wait(sync, 16 * b)
                for hf in range(2):
                    s_dl.dinc(sync.dma_start(
                        out=DLRB[:, T * hf:T * (hf + 1)],
                        in_=io["dlr_dram"][:, 2048 * (b + 4 * hf):2048 * (b + 4 * hf + 1)]))

            @block.tensor
            def _(te, b=b):
                s_dl.wait(te, 32 * (b + 1))
                for cc2 in range(2):
                    for ct8 in range(8):
                        k = 16 * b + 8 * cc2 + ct8
                        if k >= 2:
                            s_de.wait(te, k - 1)
                        s_dm.inc(te.matmul(
                            psd2[k % 2][:, :], WDT[:, 128 * cc2:128 * (cc2 + 1)],
                            DLRB[:, 512 * ct8:512 * (ct8 + 1)], start=True, stop=True), 1)

            @block.scalar
            def _(sc, b=b):
                for cc2 in range(2):
                    for ct8 in range(8):
                        k = 16 * b + 8 * cc2 + ct8
                        s_dm.wait(sc, k + 1)
                        if k >= 1:
                            s_dc.wait(sc, k)
                        sc.activation(Xsp[:, :], psd2[k % 2][:, :], AF.Identity,
                                      bias=BDT[:, cc2:cc2 + 1])
                        sc.activation(Rsp[:, :], Xsp[:, :], AF.Relu)
                        sc.activation(Asp[:, :], Xsp[:, :], AF.Abs)
                        sc.activation(Asp[:, :], Asp[:, :], AF.Exp, scale=-1.0)
                        s_de.inc(sc.activation(Asp[:, :], Asp[:, :], AF.Ln, bias=1.0), 1)

            @block.vector
            def _(v, b=b):
                for cc2 in range(2):
                    for ct8 in range(8):
                        k = 16 * b + 8 * cc2 + ct8
                        s_de.wait(v, k + 1)
                        hf, ct = ct8 // 4, ct8 % 4
                        s_dc.inc(v.tensor_tensor(
                            d16[:, cc2, T * hf + 512 * ct:T * hf + 512 * (ct + 1)],
                            Rsp[:, :], Asp[:, :], ALU.add), 1)
                    s_gx.wait(v, 16 * (4 * b + 4))
                    if b >= 1 and cc2 == 0:
                        s_yw.wait(v, 64 * b)
                    v.tensor_scalar(yb[:, cc2, :], xs16[:, cc2, :],
                                    DPT[:, cc2:cc2 + 1], None, ALU.mult)
                    s_dx.inc(v.tensor_tensor(xs16[:, cc2, :], d16[:, cc2, :],
                                             xs16[:, cc2, :], ALU.mult), 2)

            if b == 0 and "dbg_d16" in io:
                s_dbg = ctx.sem("smdbg")

                @block.gpsimd
                def _(g):
                    s_dx.wait(g, 4)
                    for cc2 in range(2):
                        s_dbg.dinc(g.dma_start(out=io["dbg_d16"][:, cc2, :], in_=d16[:, cc2, :]))
                        s_dbg.dinc(g.dma_start(out=io["dbg_dx"][:, cc2, :], in_=xs16[:, cc2, :]))
                        s_dbg.dinc(g.dma_start(out=io["dbg_xs"][:, cc2, :], in_=xs16[:, cc2, :]))
                    s_dbg.wait_abs(g)

            for n in range(16):
                m = b * 16 + n

                @block.tensor
                def _(te, b=b, n=n, m=m):
                    for kind in range(2):
                        row = n if kind == 0 else 16 + n
                        for sg in range(8):
                            kk = 16 * m + 8 * kind + sg
                            if kk >= 2:
                                s_be.wait(te, kk - 1)
                            hf, sg4 = sg // 4, sg % 4
                            col = 2048 * (b + 4 * hf) + 512 * sg4
                            s_bm.inc(te.matmul(
                                psb[kk % 2][:, :], OH[:, 128 * row:128 * (row + 1)],
                                BC16[:, col:col + 512], start=True, stop=True), 1)

                @block.scalar
                def _(sc, b=b, n=n, m=m):
                    for kind in range(2):
                        dst = Bmb[m % 2] if kind == 0 else Cmb[m % 2]
                        for sg in range(8):
                            kk = 16 * m + 8 * kind + sg
                            s_bm.wait(sc, kk + 1)
                            if m >= 2 and sg == 0:
                                if kind == 0:
                                    s_bb.wait(sc, 2 * m - 2)
                                else:
                                    s_yt.wait(sc, 2 * m - 2)
                            hf, sg4 = sg // 4, sg % 4
                            s_be.inc(sc.activation(dst[:, T * hf + 512 * sg4:T * hf + 512 * (sg4 + 1)],
                                                   psb[kk % 2][:, :], AF.Identity), 1)

                for cc2 in range(2):
                    j2i = (n * 2 + cc2) % 2
                    jj = b * 32 + n * 2 + cc2

                    @block.scalar
                    def _(sc, b=b, n=n, cc2=cc2, j2i=j2i, jj=jj):
                        s_dx.wait(sc, 4 * b + 2 * (cc2 + 1))
                        if jj >= 2:
                            s_sn.wait(sc, jj - 1)
                        s_ae.inc(sc.activation(tmp16[j2i][:, :], d16[:, cc2, :], AF.Exp,
                                               scale=AEXP[:, 16 * cc2 + n:16 * cc2 + n + 1]), 1)

                    @block.vector
                    def _(v, b=b, n=n, cc2=cc2, j2i=j2i, jj=jj):
                        s_ae.wait(v, jj + 1)
                        s_bb.wait(v, jj + 1)
                        if jj >= 2:
                            s_yt.wait(v, jj - 1)
                        s_sn.inc(v.tensor_tensor_scan(h16[j2i][:, :], tmp16[j2i][:, :],
                                                      bb16[j2i][:, :], 0.0, ALU.mult, ALU.add), 1)
                        if jj % 32 == 0:
                            tails = []
                        elif jj % 32 == 31:
                            tails = [jj - 1, jj]
                        else:
                            tails = [jj - 1]
                        for ja in tails:
                            s_yt.wait(v, ja + 1)
                            s_ya.inc(v.tensor_tensor(yb[:, ja % 2, :], yb[:, ja % 2, :],
                                                     yt16[ja % 2][:, :], ALU.add), 1)

                    @block.gpsimd
                    def _(g, b=b, n=n, cc2=cc2, jj=jj, j2i=j2i, m=m):
                        if jj % 32 == 0:
                            pre = [jj, jj + 1]
                        elif jj % 32 == 31:
                            pre = []
                        else:
                            pre = [jj + 1]
                        for k in pre:
                            kcc2, km = k % 2, k // 2
                            s_be.wait(g, 16 * km + 8)
                            s_dx.wait(g, 4 * (k // 32) + 2 * (kcc2 + 1))
                            if k >= 2:
                                s_sn.wait(g, k - 1)
                            s_bb.inc(g.tensor_tensor(bb16[k % 2][:, :], xs16[:, kcc2, :],
                                                     Bmb[km % 2][:, :], ALU.mult), 1)
                        s_be.wait(g, 16 * m + 16)
                        s_sn.wait(g, jj + 1)
                        if jj >= 2:
                            s_ya.wait(g, jj - 1)
                        s_yt.inc(g.tensor_tensor(yt16[j2i][:, :], h16[j2i][:, :],
                                                 Cmb[m % 2][:, :], ALU.mult), 1)

            @block.gpsimd
            def _(g, b=b):
                s_ya.wait(g, 32 * (b + 1))
                for cc2 in range(2):
                    for hf in range(2):
                        s_yw.dinc(g.dma_start(
                            out=io["yc"][b + 4 * hf, 128 * cc2:128 * (cc2 + 1), :],
                            in_=yb[:, cc2, T * hf:T * (hf + 1)]))

        @block.gpsimd
        def _(g):
            s_yw.wait_abs(g)
        ctx.barrier()

    s_ag2 = ctx.gsem("smag2")

    @block.gpsimd
    def _(g):
        s_ag2.inc(g.collective_compute(
            "AllGather", ALU.bypass, replica_groups=[list(range(NCORE))],
            ins=[io["yc"].ap().opt()], outs=[io["ygf"].ap().opt()]), 1)
        s_ag2.wait(g, 1)
    ctx.barrier()

    with contextlib.ExitStack() as lst:
        ych = [ctx.sb("sm_ych", [128, T], BF16, lst) for _ in range(2)]
        stg = [ctx.sb("sm_stg", [128, T], F32, lst) for _ in range(2)]
        idxu2 = ctx.sb("sm_idxu2", [128, 16], U32, lst)
        s_i2 = ctx.sem("smi2")
        s_g2 = ctx.sem("smg2")
        s_cv = ctx.sem("smcv")
        s_pa = ctx.sem("smpa")

        @block.vector
        def _(v):
            for k in range(16):
                s_, cc = k // 2, k % 2
                s_i2.inc(v.tensor_scalar(idxu2[:, k:k + 1], IDXB[:, :],
                                         2048 * s_ + 128 * cc, None, ALU.add), 1)

        @block.gpsimd
        def _(g):
            for k in range(16):
                s_i2.wait(g, k + 1)
                if k >= 2:
                    s_cv.wait(g, k - 1)
                s_g2.dinc(g.indirect_dma_start(
                    out=ych[k % 2][:, :], out_offset=None, in_=io["ygf"].ap(),
                    in_offset=IndirectOffsetOnAxis(ap=idxu2[:, k:k + 1], axis=0)))
                if k >= 1:
                    s_cv.wait(g, k)
                    s_pa.dinc(g.dma_start(out=xs_out[k - 1, :, :],
                                          in_=stg[(k - 1) % 2][:, :], accum_op=ALU.add))
            s_cv.wait(g, 16)
            s_pa.dinc(g.dma_start(out=xs_out[15, :, :], in_=stg[15 % 2][:, :],
                                  accum_op=ALU.add))
            s_pa.wait_abs(g)

        @block.scalar
        def _(sc):
            for k in range(16):
                s_g2.wait(sc, 16 * (k + 1))
                if k >= 2:
                    s_pa.wait(sc, 16 * (k - 1))
                s_cv.inc(sc.activation(stg[k % 2][:, :], ych[k % 2][:, :], AF.Identity), 1)
    ctx.barrier()



def declare_common(nc, io):
    io["consts_t0"] = nc.dram_tensor("consts", [128, 8], F32, kind="ExternalInput").ap()
    io["ones_col0"] = nc.dram_tensor("ones_col", [128, 1], BF16, kind="ExternalInput").ap()
    io["ones_row0"] = nc.dram_tensor("ones_row", [1, 128], BF16, kind="ExternalInput").ap()
    io["ident0"] = nc.dram_tensor("ident", [128, 128], BF16, kind="ExternalInput").ap()


def emit_load_common(ctx, io):
    block = ctx.block
    io["consts_t"] = ctx.sb("consts_t", [128, 8], F32)
    io["ones_col_t"] = ctx.sb("ones_colt", [128, 1], BF16)
    io["ones_row_t"] = ctx.sb("ones_rowt", [1, 128], BF16)
    io["ident_t"] = ctx.sb("identt", [128, 128], BF16)
    s_cm = ctx.sem("cmn")

    @block.sync
    def _(sync):
        s_cm.dinc(sync.dma_start(out=io["consts_t"][:, :], in_=io["consts_t0"][:, :]))
        s_cm.dinc(sync.dma_start(out=io["ones_col_t"][:, :], in_=io["ones_col0"][:, :]))
        s_cm.dinc(sync.dma_start(out=io["ones_row_t"][:, :], in_=io["ones_row0"][:, :]))
        s_cm.dinc(sync.dma_start(out=io["ident_t"][:, :], in_=io["ident0"][:, :]))
        s_cm.wait(sync, 64)
    ctx.barrier()


def build_ffn0_pass(with_dequant=True):
    nc = bass.Bass("TRN2", target_bir_lowering=False)
    io = {}
    xs_in = nc.dram_tensor("xs_in", [NCH, 128, T], F32, kind="ExternalInput").ap()
    xs_out = nc.dram_tensor("xs_out", [NCH, 128, T], F32, kind="ExternalOutput").ap()
    for nm in ("g8", "u8", "d8"):
        io[nm] = nc.dram_tensor(nm, [1024, 2048], U8, kind="ExternalInput").ap()
    for nm in ("gsc", "usc", "dsc"):
        io[nm] = nc.dram_tensor(nm, [1024, 2048], F32, kind="ExternalInput").ap()
    declare_common(nc, io)
    io["ctg"] = nc.dram_tensor("ctg", [2, 128, NCH, 1024], BF16)
    io["ctu"] = nc.dram_tensor("ctu", [2, 128, NCH, 1024], BF16)
    io["ctd"] = nc.dram_tensor("ctd", [2, 128, 8, 2048], BF16)
    io["cg"] = nc.dram_tensor("cg", [16, 128, NCH, 1024], BF16)
    io["cu"] = nc.dram_tensor("cu", [16, 128, NCH, 1024], BF16)
    io["cd"] = nc.dram_tensor("cd", [16, 128, 8, 2048], BF16)

    with contextlib.ExitStack() as stack:
        ctx = Ctx(nc, stack)
        emit_load_common(ctx, io)
        emit_stream_copy(ctx, xs_in, xs_out)
        if with_dequant:
            emit_dequant(ctx, io)
        xn = ctx.sb("xn", [128, NCH, SLAB], BF16)
        for slab in range(2):
            emit_ln(ctx, io, xs_in, xn, 0, slab * SLAB, SLAB)
            emit_ffn0_slab(ctx, io, xn, xs_out, slab * SLAB)
    return nc


def common_in_map(host):
    return {
        "consts": host["consts"], "ones_col": host["ones_col"],
        "ones_row": host["ones_row"], "ident": host["ident"],
    }


def ffn0_in_map(host, r):
    m = common_in_map(host)
    m.update({
        "g8": host["g8"][r], "u8": host["u8"][r], "d8": host["d8"][r],
        "gsc": host["gsc"][r], "usc": host["usc"][r], "dsc": host["dsc"][r],
    })
    return m


def run_ffn0_pass(host, stream):
    nc = build_ffn0_pass(with_dequant=True)
    in_maps = []
    for r in range(NCORE):
        m = ffn0_in_map(host, r)
        m["xs_in"] = stream[r]
        in_maps.append(m)
    res = run_bass_kernel_spmd(nc, in_maps, core_ids=list(range(NCORE)))
    return [np.asarray(res.results[r]["xs_out"]) for r in range(NCORE)]


def build_attn_pass():
    nc = bass.Bass("TRN2", target_bir_lowering=False)
    io = {}
    xs_in = nc.dram_tensor("xs_in", [NCH, 128, T], F32, kind="ExternalInput").ap()
    xs_out = nc.dram_tensor("xs_out", [NCH, 128, T], F32, kind="ExternalOutput").ap()
    io["wq"] = nc.dram_tensor("wq", [D, 256], F32, kind="ExternalInput").ap()
    io["wkv"] = nc.dram_tensor("wkv", [D, 128], F32, kind="ExternalInput").ap()
    io["wo"] = nc.dram_tensor("wo", [256, D], F32, kind="ExternalInput").ap()
    io["bo"] = nc.dram_tensor("bo", [128, NCH], F32, kind="ExternalInput").ap()
    io["rc"] = nc.dram_tensor("rc", [128, S], F32, kind="ExternalInput").ap()
    io["rs"] = nc.dram_tensor("rs", [128, S], F32, kind="ExternalInput").ap()
    io["slopes"] = nc.dram_tensor("slopes", [128, 4], F32, kind="ExternalInput").ap()
    io["pswapf"] = nc.dram_tensor("pswapf", [128, 128], F32, kind="ExternalInput").ap()
    declare_common(nc, io)
    io["xnc"] = nc.dram_tensor("xnc", [NCH, 128, T], BF16)
    io["xg"] = nc.dram_tensor("xg", [NCORE, NCH, 128, T], BF16)
    io["Qd"] = nc.dram_tensor("Qd", [4, 256, S], BF16)
    io["opart"] = nc.dram_tensor("opart", [NCORE, NCH, 128, T], BF16)
    io["rsout"] = nc.dram_tensor("rsout", [NCH, 128, T], BF16)

    with contextlib.ExitStack() as stack:
        ctx = Ctx(nc, stack)
        emit_load_common(ctx, io)
        emit_stream_copy(ctx, xs_in, xs_out)
        with contextlib.ExitStack() as xst:
            xn = ctx.sb("xn", [128, NCH, T], BF16, xst)
            for slab in range(2):
                emit_ln(ctx, io, xs_in, xn, slab * SLAB, slab * SLAB, SLAB)
            emit_xn_gather(ctx, io, xn, io["xnc"], io["xg"])
        emit_attention(ctx, io, xs_out)
    return nc


def run_attn_pass(host, stream):
    nc = build_attn_pass()
    in_maps = []
    for r in range(NCORE):
        m = common_in_map(host)
        m.update({"wq": host["wq"][r], "wkv": host["wkv"][r], "wo": host["wo"][r],
                  "bo": host["bo"], "rc": host["rc"], "rs": host["rs"],
                  "slopes": host["slopes"][r], "pswapf": host["pswapf"],
                  "xs_in": stream[r]})
        in_maps.append(m)
    res = run_bass_kernel_spmd(nc, in_maps, core_ids=list(range(NCORE)))
    return [np.asarray(res.results[r]["xs_out"]) for r in range(NCORE)]


def build_ssm_pass():
    nc = bass.Bass("TRN2", target_bir_lowering=False)
    io = {}
    xs_in = nc.dram_tensor("xs_in", [NCH, 128, T], F32, kind="ExternalInput").ap()
    xs_out = nc.dram_tensor("xs_out", [NCH, 128, T], F32, kind="ExternalOutput").ap()
    io["winT"] = nc.dram_tensor("winT", [D, 96], F32, kind="ExternalInput").ap()
    io["wdtT"] = nc.dram_tensor("wdtT", [64, 256], F32, kind="ExternalInput").ap()
    io["bdt"] = nc.dram_tensor("bdt", [128, 2], F32, kind="ExternalInput").ap()
    io["alogT"] = nc.dram_tensor("alogT", [128, 32], F32, kind="ExternalInput").ap()
    io["Dp"] = nc.dram_tensor("Dp", [128, 2], F32, kind="ExternalInput").ap()
    io["idxbase"] = nc.dram_tensor("idxbase", [128, 1], I32, kind="ExternalInput").ap()
    declare_common(nc, io)
    io["onehot32"] = nc.dram_tensor("onehot32", [32, 32 * 128], BF16, kind="ExternalInput").ap()
    io["xnc2"] = nc.dram_tensor("xnc2", [NCH, 128, T], BF16)
    io["xg2f"] = nc.dram_tensor("xg2f", [NCORE * D, T], BF16)
    io["dlr_dram"] = nc.dram_tensor("dlr_dram", [64, 8 * T], BF16)
    io["yc"] = nc.dram_tensor("yc", [NCORE, 256, T], BF16)
    io["ygf"] = nc.dram_tensor("ygf", [NCORE * 2048, T], BF16)

    with contextlib.ExitStack() as stack:
        ctx = Ctx(nc, stack)
        emit_load_common(ctx, io)
        emit_stream_copy(ctx, xs_in, xs_out)
        with contextlib.ExitStack() as xst:
            xn = ctx.sb("xn", [128, NCH, T], BF16, xst)
            for slab in range(2):
                emit_ln(ctx, io, xs_in, xn, slab * SLAB, slab * SLAB, SLAB)
            emit_xn_gather(ctx, io, xn, io["xnc2"], io["xg2f"])
        emit_ssm(ctx, io, xs_out)
    return nc


def run_ssm_pass(host, stream):
    nc = build_ssm_pass()
    in_maps = []
    for r in range(NCORE):
        m = common_in_map(host)
        m.update({"winT": host["winT"], "wdtT": host["wdtT"][r], "bdt": host["bdt"][r],
                  "alogT": host["alogT"][r], "Dp": host["Dp"][r],
                  "idxbase": host["idxbase"][r], "onehot32": host["onehot32"],
                  "xs_in": stream[r]})
        in_maps.append(m)
    res = run_bass_kernel_spmd(nc, in_maps, core_ids=list(range(NCORE)))
    return [np.asarray(res.results[r]["xs_out"]) for r in range(NCORE)]


def build_ffn1_pass(parts="WASTB"):
    nc = bass.Bass("TRN2", target_bir_lowering=False)
    io = {}
    xs_in = nc.dram_tensor("xs_in", [NCH, 128, T], F32, kind="ExternalInput").ap()
    xs_out = nc.dram_tensor("xs_out", [NCH, 128, T], F32, kind="ExternalOutput").ap()
    io["wg1"] = nc.dram_tensor("wg1", [D, N1], F32, kind="ExternalInput").ap()
    io["wu1"] = nc.dram_tensor("wu1", [D, N1], F32, kind="ExternalInput").ap()
    io["wd1"] = nc.dram_tensor("wd1", [N1, D], F32, kind="ExternalInput").ap()
    declare_common(nc, io)

    with contextlib.ExitStack() as stack:
        ctx = Ctx(nc, stack)
        emit_load_common(ctx, io)
        emit_stream_copy(ctx, xs_in, xs_out)
        xn = ctx.sb("xn", [128, NCH, SLAB], BF16)
        for slab in range(2):
            emit_ln(ctx, io, xs_in, xn, 0, slab * SLAB, SLAB)
            emit_ffn1_slab(ctx, io, xn, xs_out, slab * SLAB, parts=parts)
    return nc


def run_ffn1_pass(host, stream, parts="WASTB"):
    nc = build_ffn1_pass(parts)
    in_maps = []
    for r in range(NCORE):
        m = common_in_map(host)
        m.update({"wg1": host["wg1"], "wu1": host["wu1"], "wd1": host["wd1"],
                  "xs_in": stream[r]})
        in_maps.append(m)
    res = run_bass_kernel_spmd(nc, in_maps, core_ids=list(range(NCORE)))
    return [np.asarray(res.results[r]["xs_out"]) for r in range(NCORE)]



def build_fused(np_=6):
    nc = bass.Bass("TRN2", target_bir_lowering=False)
    io = {}
    xs_in = nc.dram_tensor("xs_in", [NCH, 128, T], F32, kind="ExternalInput").ap()
    xs_out = nc.dram_tensor("xs_out", [NCH, 128, T], F32, kind="ExternalOutput").ap()
    sA = nc.dram_tensor("sA", [NCH, 128, T], F32).ap()
    sB = nc.dram_tensor("sB", [NCH, 128, T], F32).ap()
    for nm in ("g8", "u8", "d8"):
        io[nm] = nc.dram_tensor(nm, [1024, 2048], U8, kind="ExternalInput").ap()
    for nm in ("gsc", "usc", "dsc"):
        io[nm] = nc.dram_tensor(nm, [1024, 2048], F32, kind="ExternalInput").ap()
    io["ctg"] = nc.dram_tensor("ctg", [2, 128, NCH, 1024], BF16)
    io["ctu"] = nc.dram_tensor("ctu", [2, 128, NCH, 1024], BF16)
    io["ctd"] = nc.dram_tensor("ctd", [2, 128, 8, 2048], BF16)
    io["cg"] = nc.dram_tensor("cg", [16, 128, NCH, 1024], BF16)
    io["cu"] = nc.dram_tensor("cu", [16, 128, NCH, 1024], BF16)
    io["cd"] = nc.dram_tensor("cd", [16, 128, 8, 2048], BF16)
    io["wq"] = nc.dram_tensor("wq", [D, 256], F32, kind="ExternalInput").ap()
    io["wkv"] = nc.dram_tensor("wkv", [D, 128], F32, kind="ExternalInput").ap()
    io["wo"] = nc.dram_tensor("wo", [256, D], F32, kind="ExternalInput").ap()
    io["bo"] = nc.dram_tensor("bo", [128, NCH], F32, kind="ExternalInput").ap()
    io["rc"] = nc.dram_tensor("rc", [128, S], F32, kind="ExternalInput").ap()
    io["rs"] = nc.dram_tensor("rs", [128, S], F32, kind="ExternalInput").ap()
    io["slopes"] = nc.dram_tensor("slopes", [128, 4], F32, kind="ExternalInput").ap()
    io["pswapf"] = nc.dram_tensor("pswapf", [128, 128], F32, kind="ExternalInput").ap()
    io["xnc"] = nc.dram_tensor("xnc", [NCH, 128, T], BF16)
    io["xg"] = nc.dram_tensor("xg", [NCORE, NCH, 128, T], BF16)
    io["Qd"] = nc.dram_tensor("Qd", [4, 256, S], BF16)
    io["opart"] = nc.dram_tensor("opart", [NCORE, NCH, 128, T], BF16)
    io["rsout"] = nc.dram_tensor("rsout", [NCH, 128, T], BF16)
    io["wg1"] = nc.dram_tensor("wg1", [D, N1], F32, kind="ExternalInput").ap()
    io["wu1"] = nc.dram_tensor("wu1", [D, N1], F32, kind="ExternalInput").ap()
    io["wd1"] = nc.dram_tensor("wd1", [N1, D], F32, kind="ExternalInput").ap()
    io["winT"] = nc.dram_tensor("winT", [D, 96], F32, kind="ExternalInput").ap()
    io["wdtT"] = nc.dram_tensor("wdtT", [64, 256], F32, kind="ExternalInput").ap()
    io["bdt"] = nc.dram_tensor("bdt", [128, 2], F32, kind="ExternalInput").ap()
    io["alogT"] = nc.dram_tensor("alogT", [128, 32], F32, kind="ExternalInput").ap()
    io["Dp"] = nc.dram_tensor("Dp", [128, 2], F32, kind="ExternalInput").ap()
    io["idxbase"] = nc.dram_tensor("idxbase", [128, 1], I32, kind="ExternalInput").ap()
    io["onehot32"] = nc.dram_tensor("onehot32", [32, 32 * 128], BF16, kind="ExternalInput").ap()
    io["xnc2"] = nc.dram_tensor("xnc2", [NCH, 128, T], BF16)
    io["xg2f"] = nc.dram_tensor("xg2f", [NCORE * D, T], BF16)
    io["dlr_dram"] = nc.dram_tensor("dlr_dram", [64, 8 * T], BF16)
    io["yc"] = nc.dram_tensor("yc", [NCORE, 256, T], BF16)
    io["ygf"] = nc.dram_tensor("ygf", [NCORE * 2048, T], BF16)
    declare_common(nc, io)

    chain = [xs_in, sA, sB, sA, sB, sA, xs_out]
    with contextlib.ExitStack() as stack:
        ctx = Ctx(nc, stack)
        emit_load_common(ctx, io)

        def scoped(fn):
            with contextlib.ExitStack() as ps_:
                ctx.default_stack = ps_
                fn()
                ctx.default_stack = None

        def pass_ffn0(cur, nxt, with_dq):
            emit_stream_copy(ctx, cur, nxt)
            if with_dq:
                emit_dequant(ctx, io)
            xn = ctx.sb("xn", [128, NCH, SLAB], BF16)
            for slab in range(2):
                emit_ln(ctx, io, cur, xn, 0, slab * SLAB, SLAB)
                emit_ffn0_slab(ctx, io, xn, nxt, slab * SLAB)

        def pass_attn(cur, nxt):
            emit_stream_copy(ctx, cur, nxt)
            with contextlib.ExitStack() as xst:
                xn = ctx.sb("xn", [128, NCH, T], BF16, xst)
                for slab in range(2):
                    emit_ln(ctx, io, cur, xn, slab * SLAB, slab * SLAB, SLAB)
                emit_xn_gather(ctx, io, xn, io["xnc"], io["xg"])
            emit_attention(ctx, io, nxt)

        def pass_ffn1(cur, nxt):
            emit_stream_copy(ctx, cur, nxt)
            xn = ctx.sb("xn", [128, NCH, SLAB], BF16)
            for slab in range(2):
                emit_ln(ctx, io, cur, xn, 0, slab * SLAB, SLAB)
                emit_ffn1_slab(ctx, io, xn, nxt, slab * SLAB, parts="WASTB")

        def pass_ssm(cur, nxt):
            emit_stream_copy(ctx, cur, nxt)
            with contextlib.ExitStack() as xst:
                xn = ctx.sb("xn", [128, NCH, T], BF16, xst)
                for slab in range(2):
                    emit_ln(ctx, io, cur, xn, slab * SLAB, slab * SLAB, SLAB)
                emit_xn_gather(ctx, io, xn, io["xnc2"], io["xg2f"])
            emit_ssm(ctx, io, nxt)

        plist = [
            lambda a, b: pass_ffn0(a, b, True),
            pass_attn,
            lambda a, b: pass_ffn0(a, b, False),
            pass_ffn1,
            pass_ssm,
            pass_ffn1,
        ]
        for i in range(np_):
            cur = chain[i]
            nxt = xs_out if i == np_ - 1 else chain[i + 1]
            scoped(lambda c=cur, x=nxt, f=plist[i]: f(c, x))
    return nc


def fused_in_map(host, r):
    m = common_in_map(host)
    m.update({
        "g8": host["g8"][r], "u8": host["u8"][r], "d8": host["d8"][r],
        "gsc": host["gsc"][r], "usc": host["usc"][r], "dsc": host["dsc"][r],
        "wq": host["wq"][r], "wkv": host["wkv"][r], "wo": host["wo"][r],
        "bo": host["bo"], "rc": host["rc"], "rs": host["rs"],
        "slopes": host["slopes"][r], "pswapf": host["pswapf"],
        "wg1": host["wg1"], "wu1": host["wu1"], "wd1": host["wd1"],
        "winT": host["winT"], "wdtT": host["wdtT"][r], "bdt": host["bdt"][r],
        "alogT": host["alogT"][r], "Dp": host["Dp"][r],
        "idxbase": host["idxbase"][r], "onehot32": host["onehot32"],
        "xs_in": host["x"][r],
    })
    return m


def kernel(**inputs):
    host = prep_host(inputs)
    nc = build_fused()
    in_maps = [fused_in_map(host, r) for r in range(NCORE)]
    res = run_bass_kernel_spmd(nc, in_maps, core_ids=list(range(NCORE)))
    stream = [np.asarray(res.results[r]["xs_out"]) for r in range(NCORE)]
    return unprep_host(stream)


def kernel_unfused(**inputs):
    host = prep_host(inputs)
    stream = host["x"]
    stream = run_ffn0_pass(host, stream)
    stream = run_attn_pass(host, stream)
    stream = run_ffn0_pass(host, stream)
    stream = run_ffn1_pass(host, stream)
    stream = run_ssm_pass(host, stream)
    stream = run_ffn1_pass(host, stream)
    return unprep_host(stream)
